# version 01
# baseline (speedup 1.0000x reference)
import numpy as np
import ml_dtypes
import concourse.bass as bass
import concourse.mybir as mybir
from concourse.bass_utils import run_bass_kernel_spmd
from concourse.alu_op_type import AluOpType as OP

F32 = mybir.dt.float32
BF16 = mybir.dt.bfloat16
AF = mybir.ActivationFunctionType
AX = mybir.AxisListType

NC_N = 8
BPC = 4
T = 2048
D = 1024
H = 16
HD = 64
E = 8
FF = 2048
NTOK = BPC * T
NCH = NTOK // 128
EPS = 1e-5
SHIFT = 10.0
MASKNEG = -1.0e6
CAP = 8192.0
BISECT_ITERS = 42

ENGS = ("pe", "act", "vec", "gp", "sy", "sc")


class Prog:
    def __init__(self):
        self.stream = {k: [] for k in ENGS}
        self.cnt = {}

    def op(self, eng, fn, waits=(), inc=None):
        if inc is not None:
            sem, n = inc
            self.cnt[sem] = self.cnt.get(sem, 0) + n
        self.stream[eng].append((fn, tuple(waits), inc))

    def c(self, sem):
        return self.cnt.get(sem, 0)

    def emit(self, nc, sems):
        with nc.Block() as block:
            def run(name, h):
                for fn, waits, inc in self.stream[name]:
                    for s, n in waits:
                        if n > 0:
                            h.wait_ge(sems[s], n)
                    ins = fn(h) if fn is not None else None
                    if inc is not None:
                        assert ins is not None
                        ins.then_inc(sems[inc[0]], inc[1])

            @block.tensor
            def _(h):
                run("pe", h)

            @block.scalar
            def _(h):
                run("sc", h)
                run("act", h)

            @block.vector
            def _(h):
                run("vec", h)

            @block.gpsimd
            def _(h):
                run("gp", h)

            @block.sync
            def _(h):
                run("sy", h)


class Builder:
    def __init__(self, dbg=()):
        self.dbg = set(dbg)
        nc = bass.Bass("TRN2", target_bir_lowering=False)
        self.nc = nc
        self.P = Prog()

        self.t_x = nc.dram_tensor("x", [BPC, T, D], F32, kind="ExternalInput")
        self.t_wqkv = nc.dram_tensor("wqkv", [D, 3 * D], F32, kind="ExternalInput")
        self.t_wproj = nc.dram_tensor("wproj", [D, D], F32, kind="ExternalInput")
        self.t_bproj = nc.dram_tensor("bproj", [1, D], F32, kind="ExternalInput")
        self.t_gam = nc.dram_tensor("gam", [1, D], F32, kind="ExternalInput")
        self.t_bet = nc.dram_tensor("bet", [1, D], F32, kind="ExternalInput")
        self.t_wg = nc.dram_tensor("wg", [D, E], F32, kind="ExternalInput")
        self.t_w1 = nc.dram_tensor("w1", [E, D, FF], F32, kind="ExternalInput")
        self.t_w2 = nc.dram_tensor("w2", [E, FF, D], F32, kind="ExternalInput")
        self.t_ranksel = nc.dram_tensor("ranksel", [E, 64], F32, kind="ExternalInput")
        self.t_y = nc.dram_tensor("y", [BPC, T, D], F32, kind="ExternalOutput")

        self.xr = self.t_x.ap().rearrange("b t d -> (b t) d")
        self.yr = self.t_y.ap().rearrange("b t d -> (b t) d")
        self.w1r = self.t_w1.ap().rearrange("e d f -> (e d) f")
        self.w2r = self.t_w2.ap().rearrange("e f d -> (e f) d")

        self.x1d = nc.dram_tensor("x1d", [NTOK, D], F32, kind="Internal")
        self.opd = nc.dram_tensor("opd", [E, NTOK], F32, kind="Internal")
        self.cc_in = nc.dram_tensor("cc_in", [64, NTOK], F32, kind="Internal")
        self.cc_out = nc.dram_tensor("cc_out", [64, NTOK], F32, kind="Internal")

        self.t_dbg = {}
        if "x1" in self.dbg:
            self.t_dbg["x1"] = nc.dram_tensor("dbg_x1", [NTOK, D], F32, kind="ExternalOutput")
        if "routing" in self.dbg:
            self.t_dbg["probs"] = nc.dram_tensor("dbg_probs", [E, NTOK], F32, kind="ExternalOutput")
            self.t_dbg["thr"] = nc.dram_tensor("dbg_thr", [E, 1], F32, kind="ExternalOutput")
            self.t_dbg["gates"] = nc.dram_tensor("dbg_gates", [128, NCH * E], F32, kind="ExternalOutput")
        if "ffn" in self.dbg:
            self.t_dbg["x1T"] = nc.dram_tensor("dbg_x1T", [128, 8 * 1024], F32, kind="ExternalOutput")
            self.t_dbg["h1T"] = nc.dram_tensor("dbg_h1T", [128, 16 * 1024], F32, kind="ExternalOutput")
            self.t_dbg["accd"] = nc.dram_tensor("dbg_accd", [128, NTOK], F32, kind="ExternalOutput")
        if "bis" in self.dbg:
            self.t_dbg["ccin"] = nc.dram_tensor("dbg_ccin", [64, NTOK], F32, kind="ExternalOutput")
            self.t_dbg["ccout"] = nc.dram_tensor("dbg_ccout", [64, NTOK], F32, kind="ExternalOutput")
            self.t_dbg["gath"] = nc.dram_tensor("dbg_gath", [128, 4096], F32, kind="ExternalOutput")
            self.t_dbg["cnt"] = nc.dram_tensor("dbg_cnt", [128, 4], F32, kind="ExternalOutput")

        self.c_identb = nc.inline_tensor(np.eye(128, dtype=ml_dtypes.bfloat16), name="identb")
        self.c_identf = nc.inline_tensor(np.eye(128, dtype=np.float32), name="identf")
        mk = np.zeros((128, 4 * 512), dtype=np.float32)
        for j in range(4):
            for k in range(128):
                q = np.arange(512)
                mk[k, j * 512:(j + 1) * 512] = np.where(j * 128 + k <= q, 0.0, MASKNEG)
        self.c_masks = nc.inline_tensor(mk, name="masks")
        selab = np.zeros((65, 128), dtype=np.float32)
        selab[64, 0:64] = 1.0
        selab[32, 64:128] = 1.0
        self.c_selab = nc.inline_tensor(selab, name="selab")
        pexp = (np.arange(128) >> 1) & 7
        selcnt = np.zeros((128, E), dtype=np.float32)
        selcnt[np.arange(128), pexp] = 1.0
        self.c_selcnt = nc.inline_tensor(selcnt, name="selcnt")
        self.c_selbc = nc.inline_tensor(np.ascontiguousarray(selcnt.T), name="selbc")
        self.c_eye8 = nc.inline_tensor(np.eye(E, dtype=np.float32), name="eye8")
        self.c_ones8x128 = nc.inline_tensor(np.ones((E, 128), dtype=np.float32), name="ones8x128")
        self.c_ones1x8 = nc.inline_tensor(np.ones((1, E), dtype=np.float32), name="ones1x8")
        self.c_ones8col = nc.inline_tensor(np.ones((E, 1), dtype=np.float32), name="ones8col")
        self.c_onescolb = nc.inline_tensor(np.ones((128, 1), dtype=ml_dtypes.bfloat16), name="onescolb")
        self.c_iotap = nc.inline_tensor(
            np.tile(np.arange(E, dtype=np.float32)[None, :] + 1000.0, (128, 1)), name="iotap")
        self.c_iota8 = nc.inline_tensor(
            np.tile(np.arange(E, dtype=np.float32)[None, :], (128, 1)), name="iota8")
        cv = np.zeros((128, 4), dtype=np.float32)
        cv[:, 0] = -SHIFT
        cv[:, 1] = EPS
        cv[:, 2] = 0.5
        cv[:, 3] = 1.0
        self.c_cvec = nc.inline_tensor(cv, name="cvec")

        A = nc.alloc_sbuf_tensor
        self.identb = A("s_identb", [128, 128], BF16)
        self.identf = A("s_identf", [128, 128], F32)
        self.selab = A("s_selab", [65, 128], F32)
        self.selcnt = A("s_selcnt", [128, E], F32)
        self.selbc = A("s_selbc", [E, 128], F32)
        self.eye8 = A("s_eye8", [E, E], F32)
        self.ones8x128 = A("s_ones8x128", [E, 128], F32)
        self.ones1x8 = A("s_ones1x8", [1, E], F32)
        self.ones8col = A("s_ones8col", [E, 1], F32)
        self.onescolb = A("s_onescolb", [128, 1], BF16)
        self.iotap = A("s_iotap", [128, E], F32)
        self.iota8 = A("s_iota8", [128, E], F32)
        self.cvec = A("s_cvec", [128, 4], F32)
        self.ranksel = A("s_ranksel", [E, 64], F32)
        self.wg = A("s_wg", [128, 8 * E], F32)
        self.bprojb = A("s_bprojb", [128, D], F32)
        self.gamb = A("s_gamb", [128, D], F32)
        self.betb = A("s_betb", [128, D], F32)
        self.gates = A("s_gates", [128, NCH * E], F32)
        self.thrb = A("s_thrb", [128, E], F32)
        self.lo8 = A("s_lo8", [E, 1], F32)
        self.hi8 = A("s_hi8", [E, 1], F32)
        self.mid8 = A("s_mid8", [E, 1], F32)
        self.tb1 = A("s_tb1", [E, 1], F32)
        self.tb2 = A("s_tb2", [E, 1], F32)
        self.ge8 = A("s_ge8", [E, 1], F32)
        self.diag8 = A("s_diag8", [E, E], F32)
        self.midb = A("s_midb", [128, 1], F32)
        self.part128 = A("s_part128", [128, 1], F32)
        self.s1 = A("s_s1", [128, 1], F32)
        self.s2 = A("s_s2", [128, 1], F32)
        self.smu = A("s_smu", [128, 1], F32)
        self.svar = A("s_svar", [128, 1], F32)
        self.srstd = A("s_srstd", [128, 1], F32)
        self.widx = A("s_widx", [128, 1], F32)
        self.wval = A("s_wval", [128, 1], F32)
        self.mmax = A("s_mmax", [128, 1], F32)
        self.dtile = A("s_dtile", [128, E], F32)
        self.dtile2 = A("s_dtile2", [128, E], F32)
        self.dtile3 = A("s_dtile3", [128, E], F32)
        self.expS = A("s_expS", [E, 128], F32)
        self.prb = A("s_prb", [E, 128], F32)
        self.rcpPr = A("s_rcpPr", [1, 128], F32)
        self.padstage = A("s_padstage", [64, 128], F32)
        self.dbgc = A("s_dbgc", [128, 4], F32)

        self.off = {}

        def at(name, shape, dtype, off):
            self.off[name] = off
            return nc.alloc_sbuf_tensor_at(name, shape, dtype, offset=off)

        A_EARLY = (8 * T * 2) + (16 * 1040 * 2) + (8 * 1024 * 2) + (2 * 8 * 512 * 2) \
            + (8 * 1024 * 2) + (2 * 1024 * 2) + (4 * 512 * 2) + (512 * 4) + (512 * 4) \
            + (4 * 512 * 4)
        A_LATE = (8 * 1024 * 2) + (2 * 1024 * 4) + 4 * 4096 + (2 * 1024 * 4)
        ARENA = A_EARLY + A_LATE
        top = nc.sbuf_top
        base = (top - ARENA) & ~63
        self.arena_base = base
        assert nc.sbuf_base + 2048 < base, (nc.sbuf_base, base)

        oa = base
        def nxtA(name, shape, dtype, nbytes):
            nonlocal oa
            t = at(name, shape, dtype, oa)
            oa += nbytes
            return t
        self.kT = nxtA("a_kT", [128, 8 * T], BF16, 8 * T * 2)
        self.v = nxtA("a_v", [128, 16 * 1040], BF16, 16 * 1040 * 2)
        self.qT = nxtA("a_qT", [128, 8 * 1024], BF16, 8 * 1024 * 2)
        self.wqkvb = nxtA("a_wqkvb", [128, 2 * 8 * 512], BF16, 2 * 8 * 512 * 2)
        self.x0T = nxtA("a_x0T", [128, 8 * 1024], BF16, 8 * 1024 * 2)
        self.x0bf = nxtA("a_x0bf", [128, 2 * 1024], BF16, 2 * 1024 * 2)
        self.expT = nxtA("a_expT", [128, 4 * 512], BF16, 4 * 512 * 2)
        self.rcp = nxtA("a_rcp", [128, 512], F32, 512 * 4)
        self.bcs = nxtA("a_bcs", [128, 512], F32, 512 * 4)
        self.masks = nxtA("a_masks", [128, 4 * 512], F32, 4 * 512 * 4)
        a_early_end = oa

        of = base
        def nxtF(name, shape, dtype, nbytes):
            nonlocal of
            t = at(name, shape, dtype, of)
            of += nbytes
            return t
        self.Xw = nxtF("f_X", [128, 8 * FF], BF16, 8 * FF * 2)
        self.Yw = nxtF("f_Y", [128, 16 * D], BF16, 16 * D * 2)
        self.h1T = nxtF("f_h1T", [128, 16 * 1024], BF16, 16 * 1024 * 2)
        self.x1T = nxtF("f_x1T", [128, 8 * 1024], BF16, 8 * 1024 * 2)
        self.x1bf = nxtF("f_x1bf", [128, 2 * 1024], BF16, 2 * 1024 * 2)
        assert of <= a_early_end, (of, a_early_end)

        ol = a_early_end
        def nxtL(name, shape, dtype, nbytes):
            nonlocal ol
            t = at(name, shape, dtype, ol)
            ol += nbytes
            return t
        self.wproj = nxtL("a_wproj", [128, 8 * 1024], BF16, 8 * 1024 * 2)
        self.x0f = nxtL("a_x0f", [128, 2 * 1024], F32, 2 * 1024 * 4)
        self.xb = nxtL("a_xb", [128, 1024], F32, 1024 * 4)
        self.ub = nxtL("a_ub", [128, 1024], F32, 1024 * 4)
        self.nb = nxtL("a_nb", [128, 1024], F32, 1024 * 4)
        self.sqb = nxtL("a_sqb", [128, 1024], F32, 1024 * 4)
        self.x1c = nxtL("a_x1c", [128, 2 * 1024], F32, 2 * 1024 * 4)
        a_late_end = ol
        self.x1tf = at("a_x1tf", [128, 8 * 128], F32, self.off["a_xb"])
        self.acc = at("f_acc", [128, NTOK], F32, a_early_end)
        of2 = a_early_end + NTOK * 4
        self.fx1f = at("f_x1f", [128, 1024], F32, of2); of2 += 4096
        self.fsq = at("f_sq", [128, 1024], F32, of2); of2 += 4096
        self.fob = at("f_ob", [128, 2 * 1024], F32, of2); of2 += 8192
        assert max(a_late_end, of2) <= base + ARENA, (a_late_end, of2, base + ARENA)
        assert base + ARENA <= top, (base, ARENA, top)

        self.pb = [nc.alloc_psum_tensor(f"pb{i}", [128, 512], F32) for i in range(7)]
        self.tpb = nc.alloc_psum_tensor("tpb", [128, 1024], BF16)
        self.pb_guard = {}

        self.semhandles = {}
        for s in ("d_gp", "d_sy", "d_sc", "s_pe", "s_act", "s_vec", "s_gp", "s_cc"):
            self.semhandles[s] = nc.semaphore(s).__enter__()

    def gpdma(self, dst, src, waits=()):
        self.P.op("gp", lambda h, d=dst, s=src: h.dma_start(out=d, in_=s),
                  waits=waits, inc=("d_gp", 16))
        return self.P.c("d_gp")

    def sydma(self, dst, src, waits=()):
        self.P.op("sy", lambda h, d=dst, s=src: h.dma_start(out=d, in_=s),
                  waits=waits, inc=("d_sy", 16))
        return self.P.c("d_sy")

    def scdma(self, dst, src, waits=()):
        self.P.op("sc", lambda h, d=dst, s=src: h.dma_start(out=d, in_=s),
                  waits=waits, inc=("d_sc", 16))
        return self.P.c("d_sc")

    def vdrain(self, inc=None):
        self.P.op("vec", lambda h: h.drain(), inc=inc)
        return self.P.c(inc[0]) if inc else None

    def adrain(self, inc=None):
        self.P.op("act", lambda h: h.drain(), inc=inc)
        return self.P.c(inc[0]) if inc else None

    def ln_stats(self, P, src_ap_fn, sqbuf):
        P.op("vec", lambda h: h.reduce_sum(self.s1[:, :], src_ap_fn(), axis=AX.X),
             inc=("s_vec", 1))
        m_s1 = P.c("s_vec")
        P.op("act", lambda h: h.activation(sqbuf[:, :], src_ap_fn(), AF.Square),
             waits=[("s_vec", m_s1)], inc=("s_act", 1))
        m_sq = P.c("s_act")
        P.op("vec", lambda h: h.reduce_sum(self.s2[:, :], sqbuf[:, :], axis=AX.X),
             waits=[("s_act", m_sq)])
        self.vdrain()
        P.op("vec", lambda h: h.tensor_scalar(self.smu[:, :], self.s1[:, :], 1.0 / D, None, OP.mult))
        P.op("vec", lambda h: h.tensor_scalar(self.svar[:, :], self.s2[:, :], 1.0 / D, None, OP.mult))
        self.vdrain()
        P.op("vec", lambda h: h.scalar_tensor_tensor(
            self.svar[:, :], self.smu[:, :], self.smu[:, :], self.svar[:, :],
            OP.mult, OP.subtract))
        m_var = self.vdrain(inc=("s_vec", 1))
        P.op("act", lambda h: h.activation(self.srstd[:, :], self.svar[:, :], AF.Ln,
                                           bias=self.cvec[:, 1:2], scale=-1.0),
             waits=[("s_vec", m_var)])
        self.adrain()
        P.op("act", lambda h: h.activation(self.srstd[:, :], self.srstd[:, :], AF.Exp,
                                           scale=-0.5))
        return self.adrain(inc=("s_act", 1))

    def plan(self):
        self.prelude()
        for s in range(BPC):
            for sl in range(2):
                self.qkv(s, sl)
                self.heads(s, sl)
                self.proj_epi(s, sl)
        self.attn_pe_end = self.P.c("s_pe")
        self.probs_phase()
        self.routing()
        for st in range(8):
            self.ffn(st)
        self.finale()

    def prelude(self):
        P = self.P
        for dst, src in (
            (self.identb[:, :], self.c_identb[:, :]),
            (self.identf[:, :], self.c_identf[:, :]),
            (self.selab[:, :], self.c_selab[:, :]),
            (self.selcnt[:, :], self.c_selcnt[:, :]),
            (self.selbc[:, :], self.c_selbc[:, :]),
            (self.eye8[:, :], self.c_eye8[:, :]),
            (self.ones8x128[:, :], self.c_ones8x128[:, :]),
            (self.ones1x8[:, :], self.c_ones1x8[:, :]),
            (self.ones8col[:, :], self.c_ones8col[:, :]),
            (self.onescolb[:, :], self.c_onescolb[:, :]),
            (self.iotap[:, :], self.c_iotap[:, :]),
            (self.iota8[:, :], self.c_iota8[:, :]),
            (self.cvec[:, :], self.c_cvec[:, :]),
            (self.masks[:, :], self.c_masks[:, :]),
            (self.ranksel[:, :], self.t_ranksel[:, :]),
            (self.bprojb[0:1, :], self.t_bproj[:, :]),
            (self.gamb[0:1, :], self.t_gam[:, :]),
            (self.betb[0:1, :], self.t_bet[:, :]),
        ):
            self.scdma(dst, src)
        for dc in range(8):
            self.scdma(self.wg[:, dc * E:(dc + 1) * E], self.t_wg.ap()[dc * 128:(dc + 1) * 128, :])
        m_consts = P.c("d_sc")

        for dc in range(8):
            self.gpdma(self.wproj[:, dc * 1024:(dc + 1) * 1024],
                       self.t_wproj.ap()[dc * 128:(dc + 1) * 128, :])
        self.m_wproj = P.c("d_gp")
        P.op("gp", lambda h: h.memset(self.kT[:, 0:4096], 0.0), inc=("s_gp", 1))
        self.gpdma(self.cc_in.ap().rearrange("a (b c) -> (a b) c", b=2),
                   self.kT[:, 0:4096], waits=[("s_gp", P.c("s_gp"))])
        self.m_cczero = P.c("d_gp")
        P.op("gp", lambda h: h.memset(
            self.v.ap().rearrange("p (t hh c) -> p t hh c", t=16, hh=16)[:, :, :, 64:65], 1.0),
            inc=("s_gp", 1))
        self.m_vones = P.c("s_gp")
        P.op("vec", lambda h: h.memset(self.rcp[:, :], 0.0), inc=("s_vec", 1))
        self.m_rcpzero = P.c("s_vec")
        P.op("vec", lambda h: h.memset(self.lo8[:, :], 0.0))
        P.op("vec", lambda h: h.memset(self.hi8[:, :], 1.0))
        P.op("vec", lambda h: h.memset(self.mid8[:, :], 0.5))
        P.op("vec", lambda h: h.memset(self.midb[:, :], 0.5), inc=("s_vec", 1))
        self.m_prelvec = P.c("s_vec")

        wc = [("d_sc", m_consts)]
        bufs = (self.bprojb, self.bprojb, self.gamb, self.gamb, self.betb, self.betb)
        for i in range(6):
            nh = i % 2
            P.op("pe", lambda h, b=bufs[i], n=nh, bk=i: h.matmul(
                self.pb[bk][:, :], self.ones8x128[0:1, :], b[0:1, n * 512:(n + 1) * 512],
                start=True, stop=True), waits=wc, inc=("s_pe", 1))
            wc = ()
        mpe = P.c("s_pe")
        for i in range(6):
            nh = i % 2
            P.op("vec", lambda h, b=bufs[i], n=nh, bk=i: h.tensor_copy(
                b[:, n * 512:(n + 1) * 512], self.pb[bk][:, :]),
                waits=[("s_pe", mpe)] if i == 0 else (),
                inc=("s_vec", 1))
        self.m_bcast_done = P.c("s_vec")
        for i in range(7):
            self.pb_guard[i] = ("s_vec", self.m_bcast_done)
        self.pb_guard[7] = ("s_vec", 0)

    def qkv(self, s, sl):
        P = self.P
        base_tok = s * T + sl * 1024
        self.x0bf_guard = getattr(self, "x0bf_guard", {})
        ch_loaded = []
        for ch in range(8):
            buf = ch % 2
            waits = []
            g = self.x0bf_guard.get(buf)
            if g:
                waits.append(g)
            m = self.gpdma(self.x0bf[:, buf * 1024:(buf + 1) * 1024],
                           self.xr[base_tok + ch * 128: base_tok + (ch + 1) * 128, :],
                           waits=waits)
            ch_loaded.append(m)
        x0T_guard = getattr(self, "m_proj_done", 0)
        for ch in range(8):
            buf = ch % 2
            waits = [("d_gp", ch_loaded[ch]), self.pb_guard[7]]
            for dc in range(8):
                P.op("pe", lambda h, b=buf, d=dc: h.transpose(
                    self.tpb[:, d * 128:(d + 1) * 128],
                    self.x0bf[:, b * 1024 + d * 128: b * 1024 + (d + 1) * 128],
                    self.identb[:, :]),
                    waits=waits if dc == 0 else (), inc=("s_pe", 1) if dc == 7 else None)
            self.x0bf_guard[buf] = ("s_pe", P.c("s_pe"))
            evw = [("s_pe", P.c("s_pe"))]
            if ch == 0:
                evw.append(("s_pe", x0T_guard))
            P.op("act", lambda h, c=ch: h.copy(
                self.x0T.ap().rearrange("p (d n) -> p d n", d=8)[:, :, c * 128:(c + 1) * 128],
                self.tpb.ap().rearrange("p (d n) -> p d n", d=8)[:, :, :]),
                waits=evw, inc=("s_act", 1))
            self.pb_guard[7] = ("s_act", P.c("s_act"))
        m_x0T_done = P.c("s_act")

        self.wqkv_guard = getattr(self, "wqkv_guard", {})
        bank = 0
        first_k_evac = True
        for wb in range(6):
            wbuf = wb % 2
            waits = []
            g = self.wqkv_guard.get(wbuf)
            if g:
                waits.append(g)
            blk_start_pe = P.c("s_pe")
            m_blk = 0
            for dc in range(8):
                m_blk = self.gpdma(
                    self.wqkvb[:, wbuf * 4096 + dc * 512: wbuf * 4096 + (dc + 1) * 512],
                    self.t_wqkv.ap()[dc * 128:(dc + 1) * 128, wb * 512:(wb + 1) * 512],
                    waits=waits if dc == 0 else ())
            is_v = wb >= 4
            if not is_v:
                for fc in range(4):
                    for th in range(2):
                        bki = bank % 4
                        bk = self.pb[bki]
                        bwaits = [("d_gp", m_blk), ("s_act", m_x0T_done), self.pb_guard[bki]]
                        for dc in range(8):
                            P.op("pe", lambda h, b=bk, w=wbuf, f=fc, t=th, d=dc: h.matmul(
                                b[:, :],
                                self.wqkvb[:, w * 4096 + d * 512 + f * 128: w * 4096 + d * 512 + (f + 1) * 128],
                                self.x0T[:, d * 1024 + t * 512: d * 1024 + (t + 1) * 512],
                                start=(d == 0), stop=(d == 7)),
                                waits=bwaits if dc == 0 else (), inc=("s_pe", 1) if dc == 7 else None)
                        mm = P.c("s_pe")
                        fg = wb * 4 + fc
                        if wb < 2:
                            dst = self.qT[:, fg * 1024 + th * 512: fg * 1024 + (th + 1) * 512]
                        else:
                            dst = self.kT[:, (fg - 8) * T + sl * 1024 + th * 512:
                                          (fg - 8) * T + sl * 1024 + (th + 1) * 512]
                        ew = [("s_pe", mm)]
                        if wb >= 2 and first_k_evac:
                            ew.append(("d_gp", self.m_cczero))
                            first_k_evac = False
                        P.op("act", lambda h, dd=dst, b=bk: h.copy(dd, b[:, :]),
                             waits=ew, inc=("s_act", 1))
                        self.pb_guard[bki] = ("s_act", P.c("s_act"))
                        bank += 1
            else:
                for tc in range(8):
                    bki = bank % 4
                    bk = self.pb[bki]
                    bwaits = [("d_gp", m_blk), ("s_act", m_x0T_done), self.pb_guard[bki]]
                    for dc in range(8):
                        P.op("pe", lambda h, b=bk, w=wbuf, t=tc, d=dc: h.matmul(
                            b[:, :],
                            self.x0T[:, d * 1024 + t * 128: d * 1024 + (t + 1) * 128],
                            self.wqkvb[:, w * 4096 + d * 512: w * 4096 + (d + 1) * 512],
                            start=(d == 0), stop=(d == 7)),
                            waits=bwaits if dc == 0 else (), inc=("s_pe", 1) if dc == 7 else None)
                    mm = P.c("s_pe")
                    tcg = sl * 8 + tc
                    hv0 = (wb - 4) * 8
                    dst = self.v.ap().rearrange("p (t hh c) -> p t hh c", t=16, hh=16)[
                        :, tcg:tcg + 1, hv0:hv0 + 8, 0:64]
                    src = bk.ap().rearrange("p (o hh c) -> p o hh c", o=1, hh=8)
                    ew = [("s_pe", mm)]
                    if wb == 4 and tc == 0:
                        ew.append(("s_gp", self.m_vones))
                    P.op("vec", lambda h, dd=dst, ss=src: h.tensor_copy(dd, ss),
                         waits=ew, inc=("s_vec", 1))
                    self.pb_guard[bki] = ("s_vec", P.c("s_vec"))
                    bank += 1
            self.wqkv_guard[wbuf] = ("s_pe", P.c("s_pe"))
        self.m_qkv_mm_done = P.c("s_pe")
        self.m_qk_evac = P.c("s_act")
        self.m_v_evac = P.c("s_vec")
        self.wqkv_guard = {0: ("s_pe", self.m_qkv_mm_done), 1: ("s_pe", self.m_qkv_mm_done)}

    def heads(self, s, sl):
        P = self.P
        self.expT_guard = getattr(self, "expT_guard", {})
        first = True
        for pair in range(8):
            for qcs in range(2):
                qc = sl * 2 + qcs
                nkc = 4 * (qc + 1)
                m_mask = 0
                for kc in range(nkc):
                    bA = self.pb[kc % 2]
                    bB = self.pb[2 + kc % 2]
                    w = [self.pb_guard[kc % 2], self.pb_guard[2 + kc % 2]]
                    if first:
                        w.append(("s_act", self.m_qk_evac))
                        w.append(("s_vec", self.m_v_evac))
                        first = False
                    P.op("pe", lambda h, b=bA, p=pair, q=qcs, k=kc: h.matmul(
                        b[:, :],
                        self.kT[0:64, p * T + k * 128: p * T + (k + 1) * 128],
                        self.qT[0:64, p * 1024 + q * 512: p * 1024 + (q + 1) * 512],
                        start=True, stop=True, tile_position=(0, 0)),
                        waits=w, inc=("s_pe", 1))
                    P.op("pe", lambda h, b=bB, p=pair, q=qcs, k=kc: h.matmul(
                        b[:, :],
                        self.kT[64:128, p * T + k * 128: p * T + (k + 1) * 128],
                        self.qT[64:128, p * 1024 + q * 512: p * 1024 + (q + 1) * 512],
                        start=True, stop=True, tile_position=(64, 0)),
                        inc=("s_pe", 1))
                    m_sc = P.c("s_pe")
                    diag = kc >= 4 * qc
                    if diag:
                        j = kc - 4 * qc
                        P.op("vec", lambda h, b=bA, jj=j: h.tensor_tensor(
                            b[:, :], b[:, :], self.masks[:, jj * 512:(jj + 1) * 512], OP.add),
                            waits=[("s_pe", m_sc)])
                        P.op("vec", lambda h, b=bB, jj=j: h.tensor_tensor(
                            b[:, :], b[:, :], self.masks[:, jj * 512:(jj + 1) * 512], OP.add),
                            inc=("s_vec", 1))
                        m_mask = P.c("s_vec")
                    ebA = kc % 2
                    ebB = 2 + kc % 2
                    ew = [("s_vec", m_mask)] if diag else [("s_pe", m_sc)]
                    for eb in (ebA, ebB):
                        g = self.expT_guard.get(eb)
                        if g:
                            ew.append(g)
                    P.op("act", lambda h, b=bA, e2=ebA: h.activation(
                        self.expT[:, e2 * 512:(e2 + 1) * 512], b[:, :], AF.Exp,
                        bias=self.cvec[:, 0:1], scale=0.125),
                        waits=ew)
                    P.op("act", lambda h, b=bB, e2=ebB: h.activation(
                        self.expT[:, e2 * 512:(e2 + 1) * 512], b[:, :], AF.Exp,
                        bias=self.cvec[:, 0:1], scale=0.125),
                        inc=("s_act", 1))
                    m_exp = P.c("s_act")
                    self.pb_guard[kc % 2] = ("s_act", m_exp)
                    self.pb_guard[2 + kc % 2] = ("s_act", m_exp)
                    vw = [("s_act", m_exp)]
                    if kc == 0:
                        vw += [self.pb_guard[4], self.pb_guard[5], self.pb_guard[6]]
                    P.op("pe", lambda h, p=pair, k=kc, e2=ebA, fk=(kc == 0), lk=(kc == nkc - 1): h.matmul(
                        self.pb[4][0:65, :],
                        self.v[:, k * 1040 + (2 * p) * 65: k * 1040 + (2 * p) * 65 + 65],
                        self.expT[:, e2 * 512:(e2 + 1) * 512],
                        start=fk, stop=lk, tile_position=(0, 0)),
                        waits=vw, inc=("s_pe", 1))
                    P.op("pe", lambda h, p=pair, k=kc, e2=ebB, fk=(kc == 0), lk=(kc == nkc - 1): h.matmul(
                        self.pb[5][64:128, :],
                        self.v[:, k * 1040 + (2 * p + 1) * 65: k * 1040 + (2 * p + 1) * 65 + 64],
                        self.expT[:, e2 * 512:(e2 + 1) * 512],
                        start=fk, stop=lk, tile_position=(0, 64)),
                        inc=("s_pe", 1))
                    P.op("pe", lambda h, k=kc, e2=ebB, fk=(kc == 0), lk=(kc == nkc - 1): h.matmul(
                        self.pb[6][32:33, :],
                        self.onescolb[:, :],
                        self.expT[:, e2 * 512:(e2 + 1) * 512],
                        start=fk, stop=lk, tile_position=(0, 32)),
                        inc=("s_pe", 1))
                    mv = P.c("s_pe")
                    self.expT_guard[ebA] = ("s_pe", mv - 2)
                    self.expT_guard[ebB] = ("s_pe", mv)
                m_att = P.c("s_pe")
                rw = [("s_pe", m_att), ("s_vec", self.m_rcpzero)]
                g = getattr(self, "rcp_guard", None)
                if g:
                    rw.append(g)
                P.op("vec", lambda h: h.reciprocal(self.rcp[64:65, :], self.pb[4][64:65, :]),
                     waits=rw)
                P.op("vec", lambda h: h.reciprocal(self.rcp[32:33, :], self.pb[6][32:33, :]))
                m_rcp = self.vdrain(inc=("s_vec", 1))
                P.op("pe", lambda h: h.matmul(
                    self.pb[6][:, :], self.selab[:, :], self.rcp[0:65, :],
                    start=True, stop=True),
                    waits=[("s_vec", m_rcp)], inc=("s_pe", 1))
                self.rcp_guard = ("s_pe", P.c("s_pe"))
                m_bc = P.c("s_pe")
                bw = [("s_pe", m_bc)]
                g = getattr(self, "bcs_guard", None)
                if g:
                    bw.append(g)
                P.op("act", lambda h: h.copy(self.bcs[:, :], self.pb[6][:, :]),
                     waits=bw, inc=("s_act", 1))
                m_bcs = P.c("s_act")
                self.pb_guard[6] = ("s_act", m_bcs)
                evw = [("s_act", m_bcs), ("s_pe", self.m_qkv_mm_done)]
                P.op("vec", lambda h, p=pair, q=qcs: h.tensor_tensor(
                    self.x0T[0:64, p * 1024 + q * 512: p * 1024 + (q + 1) * 512],
                    self.pb[4][0:64, :], self.bcs[0:64, :], OP.mult),
                    waits=evw)
                P.op("vec", lambda h, p=pair, q=qcs: h.tensor_tensor(
                    self.x0T[64:128, p * 1024 + q * 512: p * 1024 + (q + 1) * 512],
                    self.pb[5][64:128, :], self.bcs[64:128, :], OP.mult),
                    inc=("s_vec", 1))
                m_ev = P.c("s_vec")
                self.bcs_guard = ("s_vec", m_ev)
                self.pb_guard[4] = ("s_vec", m_ev)
                self.pb_guard[5] = ("s_vec", m_ev)
        self.m_heads_evac = P.c("s_vec")

    def proj_epi(self, s, sl):
        P = self.P
        self.x1w_mark = getattr(self, "x1w_mark", [0] * NCH)
        self.x0f_guard = getattr(self, "x0f_guard", {})
        self.x1c_guard = getattr(self, "x1c_guard", {})
        for tc in range(8):
            ch = s * 16 + sl * 8 + tc
            pbase = 0 if tc % 2 == 0 else 2
            fb = tc % 2
            xw = []
            g = self.x0f_guard.get(fb)
            if g:
                xw.append(g)
            m_x0 = self.sydma(self.x0f[:, fb * 1024:(fb + 1) * 1024],
                              self.xr[ch * 128:(ch + 1) * 128, :], waits=xw)
            w = [("s_vec", self.m_heads_evac), self.pb_guard[pbase], self.pb_guard[pbase + 1]]
            if not hasattr(self, "_proj_w_done"):
                w.append(("d_gp", self.m_wproj))
                self._proj_w_done = True
            for nh in range(2):
                bk = self.pb[pbase + nh]
                for dc in range(8):
                    P.op("pe", lambda h, b=bk, t=tc, n=nh, d=dc: h.matmul(
                        b[:, :],
                        self.x0T[:, d * 1024 + t * 128: d * 1024 + (t + 1) * 128],
                        self.wproj[:, d * 1024 + n * 512: d * 1024 + (n + 1) * 512],
                        start=(d == 0), stop=(d == 7)),
                        waits=w if (nh == 0 and dc == 0) else (),
                        inc=("s_pe", 1) if dc == 7 else None)
            m_proj = P.c("s_pe")
            P.op("vec", lambda h, f=fb: h.tensor_tensor(
                self.xb[:, :], self.x0f[:, f * 1024:(f + 1) * 1024], self.bprojb[:, :], OP.add),
                waits=[("d_sy", m_x0), ("s_vec", self.m_bcast_done)])
            for nh in range(2):
                P.op("vec", lambda h, n=nh, b=pbase: h.tensor_tensor(
                    self.ub[:, n * 512:(n + 1) * 512], self.pb[b + n][:, :],
                    self.xb[:, n * 512:(n + 1) * 512], OP.add),
                    waits=[("s_pe", m_proj)] if nh == 0 else (),
                    inc=("s_vec", 1) if nh == 1 else None)
            m_u = P.c("s_vec")
            self.pb_guard[pbase] = ("s_vec", m_u)
            self.pb_guard[pbase + 1] = ("s_vec", m_u)
            m_rstd = self.ln_stats(P, lambda: self.ub[:, :], self.sqb)
            P.op("vec", lambda h: h.tensor_scalar(
                self.nb[:, :], self.ub[:, :], self.smu[:, :], self.srstd[:, :],
                OP.subtract, OP.mult), waits=[("s_act", m_rstd)])
            P.op("vec", lambda h: h.tensor_tensor(self.nb[:, :], self.nb[:, :], self.gamb[:, :], OP.mult))
            P.op("vec", lambda h: h.tensor_tensor(self.nb[:, :], self.nb[:, :], self.betb[:, :], OP.add))
            P.op("vec", lambda h, f=fb: h.tensor_tensor(
                self.ub[:, :], self.nb[:, :], self.x0f[:, f * 1024:(f + 1) * 1024], OP.add),
                inc=("s_vec", 1))
            m_u2 = P.c("s_vec")
            self.x0f_guard[fb] = ("s_vec", m_u2)
            m_rstd2 = self.ln_stats(P, lambda: self.ub[:, :], self.sqb)
            cb = tc % 2
            cw = [("s_act", m_rstd2)]
            g = self.x1c_guard.get(cb)
            if g:
                cw.append(g)
            P.op("vec", lambda h, c=cb: h.tensor_scalar(
                self.x1c[:, c * 1024:(c + 1) * 1024], self.ub[:, :], self.smu[:, :], self.srstd[:, :],
                OP.subtract, OP.mult), waits=cw, inc=("s_vec", 1))
            m_x1 = P.c("s_vec")
            m_w = self.sydma(self.x1d.ap()[ch * 128:(ch + 1) * 128, :],
                             self.x1c[:, cb * 1024:(cb + 1) * 1024],
                             waits=[("s_vec", m_x1)])
            if "x1" in self.dbg:
                m_w = self.sydma(self.t_dbg["x1"].ap()[ch * 128:(ch + 1) * 128, :],
                                 self.x1c[:, cb * 1024:(cb + 1) * 1024])
            self.x1w_mark[ch] = m_w
            self.x1c_guard[cb] = ("d_sy", m_w)
        self.m_proj_done = P.c("s_pe")

    def probs_phase(self):
        P = self.P
        for ch in range(NCH):
            fb = ch % 2
            w = [("d_sy", self.x1w_mark[ch])]
            g = self.x0f_guard.get(fb)
            if g:
                w.append(g)
            m_l = self.sydma(self.x0f[:, fb * 1024:(fb + 1) * 1024],
                             self.x1d.ap()[ch * 128:(ch + 1) * 128, :], waits=w)
            tw = [("d_sy", m_l), self.pb_guard[4], self.pb_guard[5]]
            for dc in range(8):
                bk = self.pb[4 + dc // 4]
                P.op("pe", lambda h, b=bk, d=dc, f=fb: h.transpose(
                    b[:, (d % 4) * 128:((d % 4) + 1) * 128],
                    self.x0f[:, f * 1024 + d * 128: f * 1024 + (d + 1) * 128],
                    self.identf[:, :]),
                    waits=tw if dc == 0 else (), inc=("s_pe", 1) if dc == 7 else None)
            m_tp = P.c("s_pe")
            P.op("act", lambda h: h.copy(self.x1tf[:, 0:512], self.pb[4][:, :]),
                 waits=[("s_pe", m_tp)])
            P.op("act", lambda h: h.copy(self.x1tf[:, 512:1024], self.pb[5][:, :]),
                 inc=("s_act", 1))
            m_ev = P.c("s_act")
            self.pb_guard[4] = ("s_act", m_ev)
            self.pb_guard[5] = ("s_act", m_ev)
            self.x0f_guard[fb] = ("s_act", m_ev)
            rb = 6 if ch % 2 == 0 else 3
            lw = [("s_act", m_ev), self.pb_guard[rb]]
            for dc in range(8):
                P.op("pe", lambda h, d=dc, b=rb: h.matmul(
                    self.pb[b][0:8, 0:128],
                    self.wg[:, d * E:(d + 1) * E],
                    self.x1tf[:, d * 128:(d + 1) * 128],
                    start=(d == 0), stop=(d == 7)),
                    waits=lw if dc == 0 else (), inc=("s_pe", 1) if dc == 7 else None)
            m_log = P.c("s_pe")
            P.op("act", lambda h, b=rb: h.activation(self.expS[:, :], self.pb[b][0:8, 0:128], AF.Exp),
                 waits=[("s_pe", m_log)])
            m_expp = self.adrain(inc=("s_act", 1))
            P.op("pe", lambda h, b=rb: h.matmul(
                self.pb[b][0:1, 128:256], self.ones8col[:, :], self.expS[:, :],
                start=True, stop=True),
                waits=[("s_act", m_expp)], inc=("s_pe", 1))
            m_sum = P.c("s_pe")
            P.op("vec", lambda h, b=rb: h.reciprocal(self.rcpPr[:, :], self.pb[b][0:1, 128:256]),
                 waits=[("s_pe", m_sum)])
            m_rc = self.vdrain(inc=("s_vec", 1))
            P.op("pe", lambda h, b=rb: h.matmul(
                self.pb[b][0:8, 256:384], self.ones1x8[:, :], self.rcpPr[:, :],
                start=True, stop=True),
                waits=[("s_vec", m_rc)], inc=("s_pe", 1))
            m_bc = P.c("s_pe")
            P.op("vec", lambda h, b=rb: h.tensor_tensor(
                self.prb[:, :], self.expS[:, :], self.pb[b][0:8, 256:384], OP.mult),
                waits=[("s_pe", m_bc)])
            m_pr = self.vdrain(inc=("s_vec", 1))
            P.op("pe", lambda h, b=rb: h.matmul(
                self.pb[b][0:64, 384:512], self.ranksel[:, :], self.prb[:, :],
                start=True, stop=True),
                waits=[("s_vec", m_pr)], inc=("s_pe", 1))
            m_pad = P.c("s_pe")
            P.op("vec", lambda h, b=rb: h.tensor_copy(self.padstage[:, :], self.pb[b][0:64, 384:512]),
                 waits=[("s_pe", m_pad)])
            m_ps = self.vdrain(inc=("s_vec", 1))
            self.pb_guard[rb] = ("s_vec", m_ps)
            wz = [("s_vec", m_ps)]
            if ch == 0:
                wz.append(("d_gp", self.m_cczero))
            self.sydma(self.cc_in.ap()[:, ch * 128:(ch + 1) * 128], self.padstage[:, :], waits=wz)
            self.sydma(self.opd.ap()[:, ch * 128:(ch + 1) * 128], self.prb[:, :],
                       waits=[("s_vec", m_pr)])
            if "routing" in self.dbg:
                self.sydma(self.t_dbg["probs"].ap()[:, ch * 128:(ch + 1) * 128], self.prb[:, :])
        self.m_pads_done = P.c("d_sy")

    def routing(self):
        P = self.P
        P.op("gp", lambda h: h.collective_compute(
            "AllReduce", OP.add,
            replica_groups=[list(range(NC_N))],
            ins=[self.cc_in.ap().opt()],
            outs=[self.cc_out.ap().opt()]),
            waits=[("d_sy", self.m_pads_done)], inc=("s_cc", 1))
        m_cc = P.c("s_cc")
        if "bis" in self.dbg:
            self.sydma(self.t_dbg["ccin"].ap()[:, :], self.cc_in.ap()[:, :],
                       waits=[("s_cc", m_cc)])
            self.sydma(self.t_dbg["ccout"].ap()[:, :], self.cc_out.ap()[:, :])
        gath = self.acc[:, 0:4096]
        src = self.cc_out.ap().rearrange("(c e) (hh t) -> (c e hh) t", c=8, e=8, hh=2)
        m_g = self.sydma(gath, src, waits=[("s_cc", m_cc), ("s_pe", self.attn_pe_end)])
        if "bis" in self.dbg:
            m_g = self.sydma(self.t_dbg["gath"].ap()[:, :], gath, waits=[("d_sy", m_g)])
        m_oa = self.sydma(self.acc[0:8, 4096:8192], self.opd.ap()[:, 0:4096],
                          waits=[("d_sy", self.m_pads_done)])
        m_ob = self.sydma(self.acc[32:40, 4096:8192], self.opd.ap()[:, 4096:8192])
        scr = self.fob
        for it in range(BISECT_ITERS):
            w = [("d_sy", m_g), ("s_vec", self.m_prelvec)] if it == 0 else []
            P.op("vec", lambda h: h.tensor_scalar(
                scr[:, 0:2048], self.acc[:, 0:2048], self.midb[:, :], 1.0, OP.is_ge, OP.mult),
                waits=w)
            P.op("vec", lambda h: h.reduce_sum(self.part128[:, :], scr[:, 0:2048], axis=AX.X))
            P.op("vec", lambda h: h.tensor_scalar(
                scr[:, 0:2048], self.acc[:, 2048:4096], self.midb[:, :], 1.0, OP.is_ge, OP.mult))
            P.op("vec", lambda h: h.reduce_sum(self.s1[:, :], scr[:, 0:2048], axis=AX.X))
            self.vdrain()
            P.op("vec", lambda h: h.tensor_tensor(
                self.part128[:, :], self.part128[:, :], self.s1[:, :], OP.add))
            m_part = self.vdrain(inc=("s_vec", 1))
            P.op("pe", lambda h: h.matmul(
                self.pb[6][0:8, 0:1], self.selcnt[:, :], self.part128[:, :],
                start=True, stop=True),
                waits=[("s_vec", m_part)], inc=("s_pe", 1))
            m_cm = P.c("s_pe")
            if it == 0 and "bis" in self.dbg:
                P.op("vec", lambda h: h.tensor_copy(self.dbgc[:, 0:1], self.part128[:, :]))
                P.op("vec", lambda h: h.tensor_copy(self.dbgc[:, 3:4], self.midb[:, :]))
                P.op("vec", lambda h: h.tensor_copy(self.dbgc[0:8, 1:2], self.pb[6][0:8, 0:1]),
                     waits=[("s_pe", m_cm)], inc=("s_vec", 1))
                self.sydma(self.t_dbg["cnt"].ap()[:, :], self.dbgc[:, :],
                           waits=[("s_vec", P.c("s_vec"))])
            P.op("vec", lambda h: h.tensor_scalar(
                self.ge8[:, :], self.pb[6][0:8, 0:1], CAP, None, OP.is_ge),
                waits=[("s_pe", m_cm)])
            P.op("vec", lambda h: h.tensor_tensor(self.tb1[:, :], self.mid8[:, :], self.lo8[:, :], OP.subtract))
            P.op("vec", lambda h: h.tensor_tensor(self.tb2[:, :], self.hi8[:, :], self.mid8[:, :], OP.subtract))
            self.vdrain()
            P.op("vec", lambda h: h.scalar_tensor_tensor(
                self.lo8[:, :], self.tb1[:, :], self.ge8[:, :], self.lo8[:, :], OP.mult, OP.add))
            self.vdrain()
            P.op("vec", lambda h: h.scalar_tensor_tensor(
                self.hi8[:, :], self.tb2[:, :], self.ge8[:, :], self.mid8[:, :], OP.mult, OP.add))
            self.vdrain()
            P.op("vec", lambda h: h.tensor_tensor(self.mid8[:, :], self.lo8[:, :], self.hi8[:, :], OP.add))
            self.vdrain()
            P.op("vec", lambda h: h.tensor_scalar(self.mid8[:, :], self.mid8[:, :], 0.5, None, OP.mult))
            m_mid = self.vdrain(inc=("s_vec", 1))
            P.op("pe", lambda h: h.matmul(
                self.pb[6][0:128, 8:9], self.selbc[:, :], self.mid8[:, :],
                start=True, stop=True),
                waits=[("s_vec", m_mid)], inc=("s_pe", 1))
            m_mb = P.c("s_pe")
            P.op("vec", lambda h: h.tensor_copy(self.midb[:, :], self.pb[6][0:128, 8:9]),
                 waits=[("s_pe", m_mb)])
            self.vdrain(inc=("s_vec", 1))
        P.op("vec", lambda h: h.tensor_scalar(
            self.diag8[:, :], self.eye8[:, :], self.lo8[:, :], None, OP.mult))
        m_diag = self.vdrain(inc=("s_vec", 1))
        P.op("pe", lambda h: h.matmul(
            self.pb[6][0:128, 16:24], self.ones8x128[:, :], self.diag8[:, :],
            start=True, stop=True),
            waits=[("s_vec", m_diag)], inc=("s_pe", 1))
        m_thr = P.c("s_pe")
        P.op("vec", lambda h: h.tensor_copy(self.thrb[:, :], self.pb[6][0:128, 16:24]),
             waits=[("s_pe", m_thr)])
        self.vdrain(inc=("s_vec", 1))
        if "routing" in self.dbg:
            self.sydma(self.t_dbg["thr"].ap()[:, :], self.lo8[:, :],
                       waits=[("s_vec", P.c("s_vec"))])
        for ch in range(NCH):
            if ch < 32:
                srcp = self.acc[0:8, 4096 + ch * 128: 4096 + (ch + 1) * 128]
                idsl = self.identf[0:8, 0:8]
                dw = [("d_sy", m_oa)]
            else:
                srcp = self.acc[32:40, 4096 + (ch - 32) * 128: 4096 + (ch - 31) * 128]
                idsl = self.identf[32:40, 32:40]
                dw = [("d_sy", m_ob)]
            dw.append(("s_vec", P.c("s_vec")))
            P.op("pe", lambda h, sp=srcp, ii=idsl: h.transpose(
                self.pb[6][0:128, 24:32], sp, ii),
                waits=dw, inc=("s_pe", 1))
            m_t = P.c("s_pe")
            P.op("vec", lambda h: h.tensor_copy(self.dtile[:, :], self.pb[6][0:128, 24:32]),
                 waits=[("s_pe", m_t)], inc=("s_vec", 1))
            self.vdrain()
            P.op("vec", lambda h: h.tensor_tensor(self.dtile2[:, :], self.dtile[:, :], self.thrb[:, :], OP.is_ge))
            self.vdrain()
            P.op("vec", lambda h: h.tensor_tensor(self.dtile2[:, :], self.dtile2[:, :], self.dtile[:, :], OP.add))
            self.vdrain()
            P.op("vec", lambda h: h.reduce_max(self.mmax[:, :], self.dtile2[:, :], axis=AX.X))
            self.vdrain()
            P.op("vec", lambda h: h.tensor_scalar(self.dtile2[:, :], self.dtile2[:, :], self.mmax[:, :], None, OP.is_ge))
            self.vdrain()
            P.op("vec", lambda h: h.scalar_tensor_tensor(
                self.dtile3[:, :], self.dtile2[:, :], -1000.0, self.iotap[:, :], OP.mult, OP.add))
            self.vdrain()
            P.op("vec", lambda h: h.tensor_reduce(self.widx[:, :], self.dtile3[:, :], axis=AX.X, op=OP.min))
            self.vdrain()
            P.op("vec", lambda h: h.tensor_scalar(self.dtile2[:, :], self.iota8[:, :], self.widx[:, :], None, OP.is_equal))
            self.vdrain()
            P.op("vec", lambda h: h.tensor_tensor(self.dtile3[:, :], self.dtile[:, :], self.dtile2[:, :], OP.mult))
            self.vdrain()
            P.op("vec", lambda h: h.reduce_sum(self.wval[:, :], self.dtile3[:, :], axis=AX.X))
            self.vdrain()
            P.op("vec", lambda h, c=ch: h.tensor_scalar(
                self.gates[:, c * E:(c + 1) * E], self.dtile2[:, :], self.wval[:, :], None, OP.mult),
                inc=("s_vec", 1))
        self.m_gates = P.c("s_vec")
        if "routing" in self.dbg:
            self.sydma(self.t_dbg["gates"].ap()[:, :], self.gates[:, :],
                       waits=[("s_vec", self.m_gates)])

    def ffn(self, st):
        P = self.P
        self.x1bf_guard = getattr(self, "x1bf_guard", {})
        for chl in range(8):
            ch = st * 8 + chl
            fb = chl % 2
            w = [("d_sy", self.x1w_mark[ch])]
            g = self.x1bf_guard.get(fb)
            if g:
                w.append(g)
            elif chl < 2:
                w.append(("s_pe", self.attn_pe_end))
            m_l = self.gpdma(self.x1bf[:, fb * 1024:(fb + 1) * 1024],
                             self.x1d.ap()[ch * 128:(ch + 1) * 128, :], waits=w)
            tw = [("d_gp", m_l), self.pb_guard[7]]
            for dc in range(8):
                P.op("pe", lambda h, f=fb, d=dc: h.transpose(
                    self.tpb[:, d * 128:(d + 1) * 128],
                    self.x1bf[:, f * 1024 + d * 128: f * 1024 + (d + 1) * 128],
                    self.identb[:, :]),
                    waits=tw if dc == 0 else (), inc=("s_pe", 1) if dc == 7 else None)
            m_tp = P.c("s_pe")
            self.x1bf_guard[fb] = ("s_pe", m_tp)
            P.op("act", lambda h, c=chl: h.copy(
                self.x1T.ap().rearrange("p (d n) -> p d n", d=8)[:, :, c * 128:(c + 1) * 128],
                self.tpb.ap().rearrange("p (d n) -> p d n", d=8)[:, :, :]),
                waits=[("s_pe", m_tp)], inc=("s_act", 1))
            self.pb_guard[7] = ("s_act", P.c("s_act"))
        m_x1T = P.c("s_act")
        if "ffn" in self.dbg and st == 0:
            self.gpdma(self.t_dbg["x1T"].ap()[:, :], self.x1T[:, :],
                       waits=[("s_act", m_x1T)])

        for e in range(E):
            Xg = getattr(self, "X_guard", None)
            w = [Xg] if Xg else [("s_pe", self.attn_pe_end)]
            mX = 0
            for dc in range(8):
                mX = self.gpdma(self.Xw[:, dc * FF:(dc + 1) * FF],
                                self.w1r[e * D + dc * 128: e * D + (dc + 1) * 128, :],
                                waits=w if dc == 0 else ())
            Yg = getattr(self, "Y_guard", None)
            w = [Yg] if Yg else [("s_pe", self.attn_pe_end)]
            mY = 0
            for fc in range(16):
                mY = self.gpdma(self.Yw[:, fc * D:(fc + 1) * D],
                                self.w2r[e * FF + fc * 128: e * FF + (fc + 1) * 128, :],
                                waits=w if fc == 0 else ())
            h1g = getattr(self, "h1T_guard", None)
            for fc in range(16):
                for th in range(2):
                    bki = (fc * 2 + th) % 2
                    w = [self.pb_guard[bki]]
                    if fc == 0 and th == 0:
                        w += [("d_gp", mX), ("s_act", m_x1T)]
                        if h1g:
                            w.append(h1g)
                    for dc in range(8):
                        P.op("pe", lambda h, b=bki, f=fc, t=th, d=dc: h.matmul(
                            self.pb[b][:, :],
                            self.Xw[:, d * FF + f * 128: d * FF + (f + 1) * 128],
                            self.x1T[:, d * 1024 + t * 512: d * 1024 + (t + 1) * 512],
                            start=(d == 0), stop=(d == 7)),
                            waits=w if dc == 0 else (), inc=("s_pe", 1) if dc == 7 else None)
                        w = w if dc == 0 else w
                    mm = P.c("s_pe")
                    P.op("act", lambda h, b=bki, f=fc, t=th: h.activation(
                        self.h1T[:, f * 1024 + t * 512: f * 1024 + (t + 1) * 512],
                        self.pb[b][:, :], AF.Gelu),
                        waits=[("s_pe", mm)], inc=("s_act", 1))
                    self.pb_guard[bki] = ("s_act", P.c("s_act"))
            m_h1_mm = P.c("s_pe")
            m_h1 = P.c("s_act")
            if "ffn" in self.dbg and st == 0 and e == 0:
                self.gpdma(self.t_dbg["h1T"].ap()[:, :], self.h1T[:, :],
                           waits=[("s_act", m_h1)])
            self.X_guard = ("s_pe", m_h1_mm)
            for tc in range(8):
                for nh in range(2):
                    bki = 2 + (tc * 2 + nh) % 4
                    w = [self.pb_guard[bki]]
                    if tc == 0 and nh == 0:
                        w += [("d_gp", mY), ("s_act", m_h1)]
                        if e == 0:
                            w.append(("s_vec", self.m_gates))
                    for fc in range(16):
                        P.op("pe", lambda h, b=bki, t=tc, n=nh, f=fc: h.matmul(
                            self.pb[b][:, :],
                            self.h1T[:, f * 1024 + t * 128: f * 1024 + (t + 1) * 128],
                            self.Yw[:, f * 1024 + n * 512: f * 1024 + (n + 1) * 512],
                            start=(f == 0), stop=(f == 15)),
                            waits=w if fc == 0 else (), inc=("s_pe", 1) if fc == 15 else None)
                    mm = P.c("s_pe")
                    gc = (st * 8 + tc) * E + e
                    if e == 0:
                        P.op("vec", lambda h, b=bki, t=tc, n=nh, g2=gc: h.tensor_scalar(
                            self.acc[:, t * 1024 + n * 512: t * 1024 + (n + 1) * 512],
                            self.pb[b][:, :], self.gates[:, g2:g2 + 1], None, OP.mult),
                            waits=[("s_pe", mm)], inc=("s_vec", 1))
                    else:
                        P.op("vec", lambda h, b=bki, t=tc, n=nh, g2=gc: h.scalar_tensor_tensor(
                            self.acc[:, t * 1024 + n * 512: t * 1024 + (n + 1) * 512],
                            self.pb[b][:, :], self.gates[:, g2:g2 + 1],
                            self.acc[:, t * 1024 + n * 512: t * 1024 + (n + 1) * 512],
                            OP.mult, OP.add),
                            waits=[("s_pe", mm)], inc=("s_vec", 1))
                    self.pb_guard[bki] = ("s_vec", P.c("s_vec"))
            m_h2_mm = P.c("s_pe")
            self.h1T_guard = ("s_pe", m_h2_mm)
            self.Y_guard = ("s_pe", m_h2_mm)
        m_acc = P.c("s_vec")
        if "ffn" in self.dbg and st == 0:
            m_dump = self.gpdma(self.t_dbg["accd"].ap()[:, :], self.acc[:, :],
                                waits=[("s_vec", m_acc)])
            self.P.op("vec", lambda h: h.engine_nop(), waits=[("d_gp", m_dump)],
                      inc=("s_vec", 1))
            m_acc = P.c("s_vec")

        self.fx1_guard = getattr(self, "fx1_guard", None)
        self.fob_guard = getattr(self, "fob_guard", {})
        for tc in range(8):
            ch = st * 8 + tc
            w = [("d_sy", self.x1w_mark[ch])]
            if self.fx1_guard:
                w.append(self.fx1_guard)
            m_l = self.sydma(self.fx1f[:, :], self.x1d.ap()[ch * 128:(ch + 1) * 128, :], waits=w)
            P.op("vec", lambda h, t=tc: h.tensor_tensor(
                self.acc[:, t * 1024:(t + 1) * 1024],
                self.acc[:, t * 1024:(t + 1) * 1024], self.fx1f[:, :], OP.add),
                waits=[("d_sy", m_l)], inc=("s_vec", 1))
            m_u = P.c("s_vec")
            self.fx1_guard = ("s_vec", m_u)
            m_rstd = self.ln_stats(P, lambda t=tc: self.acc[:, t * 1024:(t + 1) * 1024], self.fsq)
            ob = tc % 2
            w2 = [("s_act", m_rstd)]
            g = self.fob_guard.get(ob)
            if g:
                w2.append(g)
            P.op("vec", lambda h, t=tc, o=ob: h.tensor_scalar(
                self.fob[:, o * 1024:(o + 1) * 1024],
                self.acc[:, t * 1024:(t + 1) * 1024], self.smu[:, :], self.srstd[:, :],
                OP.subtract, OP.mult), waits=w2, inc=("s_vec", 1))
            m_o = P.c("s_vec")
            m_w = self.sydma(self.yr[ch * 128:(ch + 1) * 128, :],
                             self.fob[:, ob * 1024:(ob + 1) * 1024],
                             waits=[("s_vec", m_o)])
            self.fob_guard[ob] = ("d_sy", m_w)

    def finale(self):
        P = self.P
        P.op("sy", None, waits=[("d_sy", P.c("d_sy")), ("d_gp", P.c("d_gp")),
                                ("d_sc", P.c("d_sc"))])
        P.op("gp", None, waits=[("d_gp", P.c("d_gp"))])

    def build(self):
        self.plan()
        for eng in ENGS:
            for fn, waits, inc in self.P.stream[eng]:
                for s, n in waits:
                    assert n <= self.P.c(s), (eng, s, n, self.P.c(s))
        self.P.emit(self.nc, self.semhandles)
        return self.nc


def _build(dbg=()):
    b = Builder(dbg=dbg)
    nc = b.build()
    return nc, b


def _in_maps(inputs):
    x = np.ascontiguousarray(inputs["L0_attn_target"], dtype=np.float32)
    maps = []
    for c in range(NC_N):
        ranksel = np.zeros((E, 64), dtype=np.float32)
        for e in range(E):
            ranksel[e, c * 8 + e] = 1.0
        maps.append({
            "x": np.ascontiguousarray(x[c * BPC:(c + 1) * BPC]),
            "wqkv": np.ascontiguousarray(inputs["L0_attn_w_qkv"], dtype=np.float32),
            "wproj": np.ascontiguousarray(inputs["L0_attn_w_proj"], dtype=np.float32),
            "bproj": np.ascontiguousarray(inputs["L0_attn_b_proj"], dtype=np.float32).reshape(1, D),
            "gam": np.ascontiguousarray(inputs["L0_attn_ln_gamma"], dtype=np.float32).reshape(1, D),
            "bet": np.ascontiguousarray(inputs["L0_attn_ln_beta"], dtype=np.float32).reshape(1, D),
            "wg": np.ascontiguousarray(inputs["L0_ffn_Wg"], dtype=np.float32),
            "w1": np.ascontiguousarray(inputs["L0_ffn_W1"], dtype=np.float32),
            "w2": np.ascontiguousarray(inputs["L0_ffn_W2"], dtype=np.float32),
            "ranksel": ranksel,
        })
    return maps


def _run(inputs, dbg=()):
    nc, b = _build(dbg=dbg)
    maps = _in_maps(inputs)
    res = run_bass_kernel_spmd(nc, maps, core_ids=list(range(NC_N)))
    return res


def kernel(**inputs):
    res = _run(inputs)
    out = np.concatenate([np.asarray(res.results[c]["y"]) for c in range(NC_N)], axis=0)
    return out.astype(np.float32)


# version 02 (final)
# speedup vs baseline: 1.1673x; 1.1673x over previous
import numpy as np
import ml_dtypes
import concourse.bass as bass
import concourse.mybir as mybir
from concourse.bass_utils import run_bass_kernel_spmd
from concourse.alu_op_type import AluOpType as OP

F32 = mybir.dt.float32
BF16 = mybir.dt.bfloat16
AF = mybir.ActivationFunctionType
AX = mybir.AxisListType

NC_N = 8
BPC = 4
T = 2048
D = 1024
H = 16
HD = 64
E = 8
FF = 2048
NTOK = BPC * T
NCH = NTOK // 128
EPS = 1e-5
SHIFT = 10.0
MASKNEG = -1.0e6
CAP = 8192.0
BISECT_ITERS = 36

ENGS = ("pe", "act", "vec", "gp", "sy", "sc")


class Prog:
    def __init__(self):
        self.stream = {k: [] for k in ENGS}
        self.cnt = {}

    def op(self, eng, fn, waits=(), inc=None):
        if inc is not None:
            sem, n = inc
            self.cnt[sem] = self.cnt.get(sem, 0) + n
        self.stream[eng].append((fn, tuple(waits), inc))

    def c(self, sem):
        return self.cnt.get(sem, 0)

    def emit(self, nc, sems):
        with nc.Block() as block:
            def run(name, h):
                for fn, waits, inc in self.stream[name]:
                    for s, n in waits:
                        if n > 0:
                            h.wait_ge(sems[s], n)
                    ins = fn(h) if fn is not None else None
                    if inc is not None:
                        assert ins is not None
                        ins.then_inc(sems[inc[0]], inc[1])

            @block.tensor
            def _(h):
                run("pe", h)

            @block.scalar
            def _(h):
                run("sc", h)
                run("act", h)

            @block.vector
            def _(h):
                run("vec", h)

            @block.gpsimd
            def _(h):
                run("gp", h)

            @block.sync
            def _(h):
                run("sy", h)


class Builder:
    def __init__(self, dbg=()):
        self.dbg = set(dbg)
        nc = bass.Bass("TRN2", target_bir_lowering=False)
        self.nc = nc
        self.P = Prog()

        self.t_x = nc.dram_tensor("x", [BPC, T, D], F32, kind="ExternalInput")
        self.t_wqkv = nc.dram_tensor("wqkv", [D, 3 * D], F32, kind="ExternalInput")
        self.t_wproj = nc.dram_tensor("wproj", [D, D], F32, kind="ExternalInput")
        self.t_bproj = nc.dram_tensor("bproj", [1, D], F32, kind="ExternalInput")
        self.t_gam = nc.dram_tensor("gam", [1, D], F32, kind="ExternalInput")
        self.t_bet = nc.dram_tensor("bet", [1, D], F32, kind="ExternalInput")
        self.t_wg = nc.dram_tensor("wg", [D, E], F32, kind="ExternalInput")
        self.t_w1 = nc.dram_tensor("w1", [E, D, FF], F32, kind="ExternalInput")
        self.t_w2 = nc.dram_tensor("w2", [E, FF, D], F32, kind="ExternalInput")
        self.t_ranksel = nc.dram_tensor("ranksel", [E, 64], F32, kind="ExternalInput")
        self.t_y = nc.dram_tensor("y", [BPC, T, D], F32, kind="ExternalOutput")

        self.xr = self.t_x.ap().rearrange("b t d -> (b t) d")
        self.yr = self.t_y.ap().rearrange("b t d -> (b t) d")
        self.w1r = self.t_w1.ap().rearrange("e d f -> (e d) f")
        self.w2r = self.t_w2.ap().rearrange("e f d -> (e f) d")

        self.x1d = nc.dram_tensor("x1d", [NTOK, D], F32, kind="Internal")
        self.opd = nc.dram_tensor("opd", [E, NTOK], F32, kind="Internal")
        self.cc_in = nc.dram_tensor("cc_in", [64, NTOK], F32, kind="Internal")
        self.cc_out = nc.dram_tensor("cc_out", [64, NTOK], F32, kind="Internal")

        self.t_dbg = {}
        if "x1" in self.dbg:
            self.t_dbg["x1"] = nc.dram_tensor("dbg_x1", [NTOK, D], F32, kind="ExternalOutput")
        if "routing" in self.dbg:
            self.t_dbg["probs"] = nc.dram_tensor("dbg_probs", [E, NTOK], F32, kind="ExternalOutput")
            self.t_dbg["thr"] = nc.dram_tensor("dbg_thr", [E, 1], F32, kind="ExternalOutput")
            self.t_dbg["gates"] = nc.dram_tensor("dbg_gates", [128, NCH * E], F32, kind="ExternalOutput")
        if "ffn" in self.dbg:
            self.t_dbg["x1T"] = nc.dram_tensor("dbg_x1T", [128, 8 * 1024], F32, kind="ExternalOutput")
            self.t_dbg["h1T"] = nc.dram_tensor("dbg_h1T", [128, 16 * 1024], F32, kind="ExternalOutput")
            self.t_dbg["accd"] = nc.dram_tensor("dbg_accd", [128, NTOK], F32, kind="ExternalOutput")
        if "bis" in self.dbg:
            self.t_dbg["ccin"] = nc.dram_tensor("dbg_ccin", [64, NTOK], F32, kind="ExternalOutput")
            self.t_dbg["ccout"] = nc.dram_tensor("dbg_ccout", [64, NTOK], F32, kind="ExternalOutput")
            self.t_dbg["gath"] = nc.dram_tensor("dbg_gath", [128, 4096], F32, kind="ExternalOutput")
            self.t_dbg["cnt"] = nc.dram_tensor("dbg_cnt", [128, 4], F32, kind="ExternalOutput")

        self.c_identb = nc.inline_tensor(np.eye(128, dtype=ml_dtypes.bfloat16), name="identb")
        self.c_identf = nc.inline_tensor(np.eye(128, dtype=np.float32), name="identf")
        mk = np.zeros((128, 4 * 512), dtype=np.float32)
        for j in range(4):
            for k in range(128):
                q = np.arange(512)
                mk[k, j * 512:(j + 1) * 512] = np.where(j * 128 + k <= q, 0.0, MASKNEG)
        self.c_masks = nc.inline_tensor(mk, name="masks")
        selab = np.zeros((65, 128), dtype=np.float32)
        selab[64, 0:64] = 1.0
        selab[32, 64:128] = 1.0
        self.c_selab = nc.inline_tensor(selab, name="selab")
        pexp = (np.arange(128) >> 1) & 7
        selcnt = np.zeros((128, E), dtype=np.float32)
        selcnt[np.arange(128), pexp] = 1.0
        self.c_selcnt = nc.inline_tensor(selcnt, name="selcnt")
        self.c_selbc = nc.inline_tensor(np.ascontiguousarray(selcnt.T), name="selbc")
        self.c_eye8 = nc.inline_tensor(np.eye(E, dtype=np.float32), name="eye8")
        self.c_ones8x128 = nc.inline_tensor(np.ones((E, 128), dtype=np.float32), name="ones8x128")
        self.c_ones1x8 = nc.inline_tensor(np.ones((1, E), dtype=np.float32), name="ones1x8")
        self.c_ones8col = nc.inline_tensor(np.ones((E, 1), dtype=np.float32), name="ones8col")
        self.c_onescolb = nc.inline_tensor(np.ones((128, 1), dtype=ml_dtypes.bfloat16), name="onescolb")
        self.c_iotap = nc.inline_tensor(
            np.tile(np.arange(E, dtype=np.float32)[None, :] + 1000.0, (128, 1)), name="iotap")
        self.c_iota8 = nc.inline_tensor(
            np.tile(np.arange(E, dtype=np.float32)[None, :], (128, 1)), name="iota8")
        cv = np.zeros((128, 4), dtype=np.float32)
        cv[:, 0] = -SHIFT
        cv[:, 1] = EPS
        cv[:, 2] = 0.5
        cv[:, 3] = 1.0
        self.c_cvec = nc.inline_tensor(cv, name="cvec")

        A = nc.alloc_sbuf_tensor
        self.identb = A("s_identb", [128, 128], BF16)
        self.identf = A("s_identf", [128, 128], F32)
        self.selab = A("s_selab", [65, 128], F32)
        self.selcnt = A("s_selcnt", [128, E], F32)
        self.selbc = A("s_selbc", [E, 128], F32)
        self.eye8 = A("s_eye8", [E, E], F32)
        self.ones8x128 = A("s_ones8x128", [E, 128], F32)
        self.ones1x8 = A("s_ones1x8", [1, E], F32)
        self.ones8col = A("s_ones8col", [E, 1], F32)
        self.onescolb = A("s_onescolb", [128, 1], BF16)
        self.iotap = A("s_iotap", [128, E], F32)
        self.iota8 = A("s_iota8", [128, E], F32)
        self.cvec = A("s_cvec", [128, 4], F32)
        self.ranksel = A("s_ranksel", [E, 64], F32)
        self.wg = A("s_wg", [128, 8 * E], F32)
        self.bprojb = A("s_bprojb", [128, D], F32)
        self.gamb = A("s_gamb", [128, D], F32)
        self.betb = A("s_betb", [128, D], F32)
        self.gates = A("s_gates", [128, NCH * E], F32)
        self.thrb = A("s_thrb", [128, E], F32)
        self.lo8 = A("s_lo8", [E, 1], F32)
        self.hi8 = A("s_hi8", [E, 1], F32)
        self.mid8 = A("s_mid8", [E, 1], F32)
        self.tb1 = A("s_tb1", [E, 1], F32)
        self.tb2 = A("s_tb2", [E, 1], F32)
        self.ge8 = A("s_ge8", [E, 1], F32)
        self.diag8 = A("s_diag8", [E, E], F32)
        self.midb = A("s_midb", [128, 1], F32)
        self.part128 = A("s_part128", [128, 1], F32)
        self.s1 = A("s_s1", [128, 1], F32)
        self.s2 = A("s_s2", [128, 1], F32)
        self.smu = A("s_smu", [128, 1], F32)
        self.svar = A("s_svar", [128, 1], F32)
        self.srstd = A("s_srstd", [128, 1], F32)
        self.widx = A("s_widx", [128, 1], F32)
        self.wval = A("s_wval", [128, 1], F32)
        self.mmax = A("s_mmax", [128, 1], F32)
        self.dtile = A("s_dtile", [128, E], F32)
        self.dtile2 = A("s_dtile2", [128, E], F32)
        self.dtile3 = A("s_dtile3", [128, E], F32)
        self.expS = A("s_expS", [E, 128], F32)
        self.prb = A("s_prb", [E, 128], F32)
        self.rcpPr = A("s_rcpPr", [1, 128], F32)
        self.padstage = A("s_padstage", [64, 128], F32)
        self.dbgc = A("s_dbgc", [128, 4], F32)

        self.off = {}

        def at(name, shape, dtype, off):
            self.off[name] = off
            return nc.alloc_sbuf_tensor_at(name, shape, dtype, offset=off)

        A_EARLY = (8 * T * 2) + (16 * 1040 * 2) + (8 * 1024 * 2) + (2 * 8 * 512 * 2) \
            + (8 * 1024 * 2) + (2 * 1024 * 2) + (4 * 512 * 2) + (512 * 4) + (512 * 4) \
            + (4 * 512 * 4)
        A_LATE = (8 * 1024 * 2) + (2 * 1024 * 4) + 4 * 4096 + (2 * 1024 * 4)
        ARENA = A_EARLY + A_LATE
        top = nc.sbuf_top
        base = (top - ARENA) & ~63
        self.arena_base = base
        assert nc.sbuf_base + 2048 < base, (nc.sbuf_base, base)

        oa = base
        def nxtA(name, shape, dtype, nbytes):
            nonlocal oa
            t = at(name, shape, dtype, oa)
            oa += nbytes
            return t
        self.kT = nxtA("a_kT", [128, 8 * T], BF16, 8 * T * 2)
        self.v = nxtA("a_v", [128, 16 * 1040], BF16, 16 * 1040 * 2)
        self.qT = nxtA("a_qT", [128, 8 * 1024], BF16, 8 * 1024 * 2)
        self.wqkvb = nxtA("a_wqkvb", [128, 2 * 8 * 512], BF16, 2 * 8 * 512 * 2)
        self.x0T = nxtA("a_x0T", [128, 8 * 1024], BF16, 8 * 1024 * 2)
        self.x0bf = nxtA("a_x0bf", [128, 2 * 1024], BF16, 2 * 1024 * 2)
        self.expT = nxtA("a_expT", [128, 4 * 512], BF16, 4 * 512 * 2)
        self.rcp = nxtA("a_rcp", [128, 512], F32, 512 * 4)
        self.bcs = nxtA("a_bcs", [128, 512], F32, 512 * 4)
        self.masks = nxtA("a_masks", [128, 4 * 512], F32, 4 * 512 * 4)
        a_early_end = oa

        of = base
        def nxtF(name, shape, dtype, nbytes):
            nonlocal of
            t = at(name, shape, dtype, of)
            of += nbytes
            return t
        self.Xw = nxtF("f_X", [128, 8 * FF], BF16, 8 * FF * 2)
        self.Yw = nxtF("f_Y", [128, 16 * D], BF16, 16 * D * 2)
        self.h1T = nxtF("f_h1T", [128, 16 * 1024], BF16, 16 * 1024 * 2)
        self.x1T = nxtF("f_x1T", [128, 8 * 1024], BF16, 8 * 1024 * 2)
        self.x1bf = nxtF("f_x1bf", [128, 2 * 1024], BF16, 2 * 1024 * 2)
        assert of <= a_early_end, (of, a_early_end)

        ol = a_early_end
        def nxtL(name, shape, dtype, nbytes):
            nonlocal ol
            t = at(name, shape, dtype, ol)
            ol += nbytes
            return t
        self.wproj = nxtL("a_wproj", [128, 8 * 1024], BF16, 8 * 1024 * 2)
        self.x0f = nxtL("a_x0f", [128, 2 * 1024], F32, 2 * 1024 * 4)
        self.xb = nxtL("a_xb", [128, 1024], F32, 1024 * 4)
        self.ub = nxtL("a_ub", [128, 1024], F32, 1024 * 4)
        self.nb = nxtL("a_nb", [128, 1024], F32, 1024 * 4)
        self.sqb = nxtL("a_sqb", [128, 1024], F32, 1024 * 4)
        self.x1c = nxtL("a_x1c", [128, 2 * 1024], F32, 2 * 1024 * 4)
        a_late_end = ol
        self.x1tf = at("a_x1tf", [128, 8 * 128], F32, self.off["a_xb"])
        self.acc = at("f_acc", [128, NTOK], F32, a_early_end)
        of2 = a_early_end + NTOK * 4
        self.fx1f = at("f_x1f", [128, 1024], F32, of2); of2 += 4096
        self.fsq = at("f_sq", [128, 1024], F32, of2); of2 += 4096
        self.fob = at("f_ob", [128, 2 * 1024], F32, of2); of2 += 8192
        assert max(a_late_end, of2) <= base + ARENA, (a_late_end, of2, base + ARENA)
        assert base + ARENA <= top, (base, ARENA, top)

        self.pb = [nc.alloc_psum_tensor(f"pb{i}", [128, 512], F32) for i in range(7)]
        self.tpb = nc.alloc_psum_tensor("tpb", [128, 1024], BF16)
        self.pb_guard = {}

        self.semhandles = {}
        for s in ("d_gp", "d_sy", "d_sc", "s_pe", "s_act", "s_vec", "s_gp", "s_cc"):
            self.semhandles[s] = nc.semaphore(s).__enter__()

    def gpdma(self, dst, src, waits=()):
        self.P.op("gp", lambda h, d=dst, s=src: h.dma_start(out=d, in_=s),
                  waits=waits, inc=("d_gp", 16))
        return self.P.c("d_gp")

    def sydma(self, dst, src, waits=()):
        self.P.op("sy", lambda h, d=dst, s=src: h.dma_start(out=d, in_=s),
                  waits=waits, inc=("d_sy", 16))
        return self.P.c("d_sy")

    def scdma(self, dst, src, waits=()):
        self.P.op("sc", lambda h, d=dst, s=src: h.dma_start(out=d, in_=s),
                  waits=waits, inc=("d_sc", 16))
        return self.P.c("d_sc")

    def vdrain(self, inc=None):
        self.P.op("vec", lambda h: h.drain(), inc=inc)
        return self.P.c(inc[0]) if inc else None

    def adrain(self, inc=None):
        self.P.op("act", lambda h: h.drain(), inc=inc)
        return self.P.c(inc[0]) if inc else None

    def ln_stats(self, P, src_ap_fn, sqbuf):
        P.op("vec", lambda h: h.reduce_sum(self.s1[:, :], src_ap_fn(), axis=AX.X),
             inc=("s_vec", 1))
        m_s1 = P.c("s_vec")
        P.op("act", lambda h: h.activation(sqbuf[:, :], src_ap_fn(), AF.Square),
             waits=[("s_vec", m_s1)], inc=("s_act", 1))
        m_sq = P.c("s_act")
        P.op("vec", lambda h: h.reduce_sum(self.s2[:, :], sqbuf[:, :], axis=AX.X),
             waits=[("s_act", m_sq)])
        self.vdrain()
        P.op("vec", lambda h: h.tensor_scalar(self.smu[:, :], self.s1[:, :], 1.0 / D, None, OP.mult))
        P.op("vec", lambda h: h.tensor_scalar(self.svar[:, :], self.s2[:, :], 1.0 / D, None, OP.mult))
        self.vdrain()
        P.op("vec", lambda h: h.scalar_tensor_tensor(
            self.svar[:, :], self.smu[:, :], self.smu[:, :], self.svar[:, :],
            OP.mult, OP.subtract))
        m_var = self.vdrain(inc=("s_vec", 1))
        P.op("act", lambda h: h.activation(self.srstd[:, :], self.svar[:, :], AF.Ln,
                                           bias=self.cvec[:, 1:2], scale=-1.0),
             waits=[("s_vec", m_var)])
        self.adrain()
        P.op("act", lambda h: h.activation(self.srstd[:, :], self.srstd[:, :], AF.Exp,
                                           scale=-0.5))
        return self.adrain(inc=("s_act", 1))

    def plan(self):
        self.prelude()
        for s in range(BPC):
            for sl in range(2):
                self.qkv(s, sl)
                self.heads(s, sl)
                self.proj_epi(s, sl)
                for tc in range(8):
                    self.probs_chunk(s * 16 + sl * 8 + tc)
        self.attn_pe_end = self.P.c("s_pe")
        self.ffn_prep(0)
        for st in range(8):
            self.ffn_body(st, mid=self.routing if st == 0 else None)
        self.finale()

    def prelude(self):
        P = self.P
        for dst, src in (
            (self.identb[:, :], self.c_identb[:, :]),
            (self.identf[:, :], self.c_identf[:, :]),
            (self.selab[:, :], self.c_selab[:, :]),
            (self.selcnt[:, :], self.c_selcnt[:, :]),
            (self.selbc[:, :], self.c_selbc[:, :]),
            (self.eye8[:, :], self.c_eye8[:, :]),
            (self.ones8x128[:, :], self.c_ones8x128[:, :]),
            (self.ones1x8[:, :], self.c_ones1x8[:, :]),
            (self.ones8col[:, :], self.c_ones8col[:, :]),
            (self.onescolb[:, :], self.c_onescolb[:, :]),
            (self.iotap[:, :], self.c_iotap[:, :]),
            (self.iota8[:, :], self.c_iota8[:, :]),
            (self.cvec[:, :], self.c_cvec[:, :]),
            (self.masks[:, :], self.c_masks[:, :]),
            (self.ranksel[:, :], self.t_ranksel[:, :]),
            (self.bprojb[0:1, :], self.t_bproj[:, :]),
            (self.gamb[0:1, :], self.t_gam[:, :]),
            (self.betb[0:1, :], self.t_bet[:, :]),
        ):
            self.scdma(dst, src)
        for dc in range(8):
            self.scdma(self.wg[:, dc * E:(dc + 1) * E], self.t_wg.ap()[dc * 128:(dc + 1) * 128, :])
        m_consts = P.c("d_sc")

        for dc in range(8):
            self.gpdma(self.wproj[:, dc * 1024:(dc + 1) * 1024],
                       self.t_wproj.ap()[dc * 128:(dc + 1) * 128, :])
        self.m_wproj = P.c("d_gp")
        P.op("gp", lambda h: h.memset(self.kT[:, 0:4096], 0.0), inc=("s_gp", 1))
        self.gpdma(self.cc_in.ap().rearrange("a (b c) -> (a b) c", b=2),
                   self.kT[:, 0:4096], waits=[("s_gp", P.c("s_gp"))])
        self.m_cczero = P.c("d_gp")
        P.op("gp", lambda h: h.memset(
            self.v.ap().rearrange("p (t hh c) -> p t hh c", t=16, hh=16)[:, :, :, 64:65], 1.0),
            inc=("s_gp", 1))
        self.m_vones = P.c("s_gp")
        P.op("vec", lambda h: h.memset(self.rcp[:, :], 0.0), inc=("s_vec", 1))
        self.m_rcpzero = P.c("s_vec")
        P.op("vec", lambda h: h.memset(self.lo8[:, :], 0.0))
        P.op("vec", lambda h: h.memset(self.hi8[:, :], 1.0))
        P.op("vec", lambda h: h.memset(self.mid8[:, :], 0.5))
        P.op("vec", lambda h: h.memset(self.midb[:, :], 0.5), inc=("s_vec", 1))
        self.m_prelvec = P.c("s_vec")

        wc = [("d_sc", m_consts)]
        bufs = (self.bprojb, self.bprojb, self.gamb, self.gamb, self.betb, self.betb)
        for i in range(6):
            nh = i % 2
            P.op("pe", lambda h, b=bufs[i], n=nh, bk=i: h.matmul(
                self.pb[bk][:, :], self.ones8x128[0:1, :], b[0:1, n * 512:(n + 1) * 512],
                start=True, stop=True), waits=wc, inc=("s_pe", 1))
            wc = ()
        mpe = P.c("s_pe")
        for i in range(6):
            nh = i % 2
            P.op("vec", lambda h, b=bufs[i], n=nh, bk=i: h.tensor_copy(
                b[:, n * 512:(n + 1) * 512], self.pb[bk][:, :]),
                waits=[("s_pe", mpe)] if i == 0 else (),
                inc=("s_vec", 1))
        self.m_bcast_done = P.c("s_vec")
        for i in range(7):
            self.pb_guard[i] = ("s_vec", self.m_bcast_done)
        self.pb_guard[7] = ("s_vec", 0)

    def qkv(self, s, sl):
        P = self.P
        base_tok = s * T + sl * 1024
        self.x0bf_guard = getattr(self, "x0bf_guard", {})
        ch_loaded = []
        for ch in range(8):
            buf = ch % 2
            waits = []
            g = self.x0bf_guard.get(buf)
            if g:
                waits.append(g)
            m = self.gpdma(self.x0bf[:, buf * 1024:(buf + 1) * 1024],
                           self.xr[base_tok + ch * 128: base_tok + (ch + 1) * 128, :],
                           waits=waits)
            ch_loaded.append(m)
        x0T_guard = getattr(self, "m_proj_done", 0)
        for ch in range(8):
            buf = ch % 2
            waits = [("d_gp", ch_loaded[ch]), self.pb_guard[7]]
            for dc in range(8):
                P.op("pe", lambda h, b=buf, d=dc: h.transpose(
                    self.tpb[:, d * 128:(d + 1) * 128],
                    self.x0bf[:, b * 1024 + d * 128: b * 1024 + (d + 1) * 128],
                    self.identb[:, :]),
                    waits=waits if dc == 0 else (), inc=("s_pe", 1) if dc == 7 else None)
            self.x0bf_guard[buf] = ("s_pe", P.c("s_pe"))
            evw = [("s_pe", P.c("s_pe"))]
            if ch == 0:
                evw.append(("s_pe", x0T_guard))
            P.op("act", lambda h, c=ch: h.copy(
                self.x0T.ap().rearrange("p (d n) -> p d n", d=8)[:, :, c * 128:(c + 1) * 128],
                self.tpb.ap().rearrange("p (d n) -> p d n", d=8)[:, :, :]),
                waits=evw, inc=("s_act", 1))
            self.pb_guard[7] = ("s_act", P.c("s_act"))
            if ch == 3:
                m_x0T_h0 = P.c("s_act")
        m_x0T_done = P.c("s_act")
        m_x0T_half = [m_x0T_h0, m_x0T_done]

        self.wqkv_guard = getattr(self, "wqkv_guard", {})
        bank = 0
        first_k_evac = True
        for wb in range(6):
            wbuf = wb % 2
            waits = []
            g = self.wqkv_guard.get(wbuf)
            if g:
                waits.append(g)
            blk_start_pe = P.c("s_pe")
            m_blk = 0
            for dc in range(8):
                m_blk = self.gpdma(
                    self.wqkvb[:, wbuf * 4096 + dc * 512: wbuf * 4096 + (dc + 1) * 512],
                    self.t_wqkv.ap()[dc * 128:(dc + 1) * 128, wb * 512:(wb + 1) * 512],
                    waits=waits if dc == 0 else ())
            is_v = wb >= 4
            if not is_v:
                for fc in range(4):
                    for th in range(2):
                        bki = bank % 4
                        bk = self.pb[bki]
                        bwaits = [("d_gp", m_blk), ("s_act", m_x0T_half[th]), self.pb_guard[bki]]
                        for dc in range(8):
                            P.op("pe", lambda h, b=bk, w=wbuf, f=fc, t=th, d=dc: h.matmul(
                                b[:, :],
                                self.wqkvb[:, w * 4096 + d * 512 + f * 128: w * 4096 + d * 512 + (f + 1) * 128],
                                self.x0T[:, d * 1024 + t * 512: d * 1024 + (t + 1) * 512],
                                start=(d == 0), stop=(d == 7)),
                                waits=bwaits if dc == 0 else (), inc=("s_pe", 1) if dc == 7 else None)
                        mm = P.c("s_pe")
                        fg = wb * 4 + fc
                        if wb < 2:
                            dst = self.qT[:, fg * 1024 + th * 512: fg * 1024 + (th + 1) * 512]
                        else:
                            dst = self.kT[:, (fg - 8) * T + sl * 1024 + th * 512:
                                          (fg - 8) * T + sl * 1024 + (th + 1) * 512]
                        ew = [("s_pe", mm)]
                        if wb >= 2 and first_k_evac:
                            ew.append(("d_gp", self.m_cczero))
                            first_k_evac = False
                        P.op("act", lambda h, dd=dst, b=bk: h.copy(dd, b[:, :]),
                             waits=ew, inc=("s_act", 1))
                        self.pb_guard[bki] = ("s_act", P.c("s_act"))
                        bank += 1
            else:
                for tc in range(8):
                    bki = bank % 4
                    bk = self.pb[bki]
                    bwaits = [("d_gp", m_blk), ("s_act", m_x0T_half[tc // 4]), self.pb_guard[bki]]
                    for dc in range(8):
                        P.op("pe", lambda h, b=bk, w=wbuf, t=tc, d=dc: h.matmul(
                            b[:, :],
                            self.x0T[:, d * 1024 + t * 128: d * 1024 + (t + 1) * 128],
                            self.wqkvb[:, w * 4096 + d * 512: w * 4096 + (d + 1) * 512],
                            start=(d == 0), stop=(d == 7)),
                            waits=bwaits if dc == 0 else (), inc=("s_pe", 1) if dc == 7 else None)
                    mm = P.c("s_pe")
                    tcg = sl * 8 + tc
                    hv0 = (wb - 4) * 8
                    dst = self.v.ap().rearrange("p (t hh c) -> p t hh c", t=16, hh=16)[
                        :, tcg:tcg + 1, hv0:hv0 + 8, 0:64]
                    src = bk.ap().rearrange("p (o hh c) -> p o hh c", o=1, hh=8)
                    ew = [("s_pe", mm)]
                    if wb == 4 and tc == 0:
                        ew.append(("s_gp", self.m_vones))
                    P.op("vec", lambda h, dd=dst, ss=src: h.tensor_copy(dd, ss),
                         waits=ew, inc=("s_vec", 1))
                    self.pb_guard[bki] = ("s_vec", P.c("s_vec"))
                    bank += 1
            self.wqkv_guard[wbuf] = ("s_pe", P.c("s_pe"))
        self.m_qkv_mm_done = P.c("s_pe")
        self.m_qk_evac = P.c("s_act")
        self.m_v_evac = P.c("s_vec")
        self.wqkv_guard = {0: ("s_pe", self.m_qkv_mm_done), 1: ("s_pe", self.m_qkv_mm_done)}

    def heads(self, s, sl):
        P = self.P
        self.expT_guard = getattr(self, "expT_guard", {})
        first = True
        for pair in range(8):
            for qcs in range(2):
                qc = sl * 2 + qcs
                nkc = 4 * (qc + 1)
                m_mask = 0
                pend_v = None

                def emit_v(pair_, kc_, ebA_, ebB_, fk_, lk_, vw_):
                    P.op("pe", lambda h, p=pair_, k=kc_, e2=ebA_, fk=fk_, lk=lk_: h.matmul(
                        self.pb[4][0:65, :],
                        self.v[:, k * 1040 + (2 * p) * 65: k * 1040 + (2 * p) * 65 + 65],
                        self.expT[:, e2 * 512:(e2 + 1) * 512],
                        start=fk, stop=lk, tile_position=(0, 0)),
                        waits=vw_, inc=("s_pe", 1))
                    P.op("pe", lambda h, p=pair_, k=kc_, e2=ebB_, fk=fk_, lk=lk_: h.matmul(
                        self.pb[5][64:128, :],
                        self.v[:, k * 1040 + (2 * p + 1) * 65: k * 1040 + (2 * p + 1) * 65 + 64],
                        self.expT[:, e2 * 512:(e2 + 1) * 512],
                        start=fk, stop=lk, tile_position=(0, 64)),
                        inc=("s_pe", 1))
                    P.op("pe", lambda h, k=kc_, e2=ebB_, fk=fk_, lk=lk_: h.matmul(
                        self.pb[6][32:33, :],
                        self.onescolb[:, :],
                        self.expT[:, e2 * 512:(e2 + 1) * 512],
                        start=fk, stop=lk, tile_position=(0, 32)),
                        inc=("s_pe", 1))
                    mv = P.c("s_pe")
                    self.expT_guard[ebA_] = ("s_pe", mv - 2)
                    self.expT_guard[ebB_] = ("s_pe", mv)

                for kc in range(nkc):
                    bA = self.pb[kc % 2]
                    bB = self.pb[2 + kc % 2]
                    w = [self.pb_guard[kc % 2], self.pb_guard[2 + kc % 2]]
                    if first:
                        w.append(("s_act", self.m_qk_evac))
                        w.append(("s_vec", self.m_v_evac))
                        first = False
                    P.op("pe", lambda h, b=bA, p=pair, q=qcs, k=kc: h.matmul(
                        b[:, :],
                        self.kT[0:64, p * T + k * 128: p * T + (k + 1) * 128],
                        self.qT[0:64, p * 1024 + q * 512: p * 1024 + (q + 1) * 512],
                        start=True, stop=True, tile_position=(0, 0)),
                        waits=w, inc=("s_pe", 1))
                    P.op("pe", lambda h, b=bB, p=pair, q=qcs, k=kc: h.matmul(
                        b[:, :],
                        self.kT[64:128, p * T + k * 128: p * T + (k + 1) * 128],
                        self.qT[64:128, p * 1024 + q * 512: p * 1024 + (q + 1) * 512],
                        start=True, stop=True, tile_position=(64, 0)),
                        inc=("s_pe", 1))
                    m_sc = P.c("s_pe")
                    diag = kc >= 4 * qc
                    if diag:
                        j = kc - 4 * qc
                        P.op("vec", lambda h, b=bA, jj=j: h.tensor_tensor(
                            b[:, :], b[:, :], self.masks[:, jj * 512:(jj + 1) * 512], OP.add),
                            waits=[("s_pe", m_sc)])
                        P.op("vec", lambda h, b=bB, jj=j: h.tensor_tensor(
                            b[:, :], b[:, :], self.masks[:, jj * 512:(jj + 1) * 512], OP.add),
                            inc=("s_vec", 1))
                        m_mask = P.c("s_vec")
                    ebA = kc % 2
                    ebB = 2 + kc % 2
                    ew = [("s_vec", m_mask)] if diag else [("s_pe", m_sc)]
                    for eb in (ebA, ebB):
                        g = self.expT_guard.get(eb)
                        if g:
                            ew.append(g)
                    P.op("act", lambda h, b=bA, e2=ebA: h.activation(
                        self.expT[:, e2 * 512:(e2 + 1) * 512], b[:, :], AF.Exp,
                        bias=self.cvec[:, 0:1], scale=0.125),
                        waits=ew)
                    P.op("act", lambda h, b=bB, e2=ebB: h.activation(
                        self.expT[:, e2 * 512:(e2 + 1) * 512], b[:, :], AF.Exp,
                        bias=self.cvec[:, 0:1], scale=0.125),
                        inc=("s_act", 1))
                    m_exp = P.c("s_act")
                    self.pb_guard[kc % 2] = ("s_act", m_exp)
                    self.pb_guard[2 + kc % 2] = ("s_act", m_exp)
                    vw = [("s_act", m_exp)]
                    if kc == 0:
                        vw += [self.pb_guard[4], self.pb_guard[5], self.pb_guard[6]]
                    if pend_v is not None:
                        pend_v()
                    pend_v = (lambda p2=pair, k2=kc, eA=ebA, eB=ebB, fk2=(kc == 0),
                              lk2=(kc == nkc - 1), vw2=tuple(vw):
                              emit_v(p2, k2, eA, eB, fk2, lk2, vw2))
                pend_v()
                m_att = P.c("s_pe")
                rw = [("s_pe", m_att), ("s_vec", self.m_rcpzero)]
                g = getattr(self, "rcp_guard", None)
                if g:
                    rw.append(g)
                P.op("vec", lambda h: h.reciprocal(self.rcp[64:65, :], self.pb[4][64:65, :]),
                     waits=rw)
                P.op("vec", lambda h: h.reciprocal(self.rcp[32:33, :], self.pb[6][32:33, :]))
                m_rcp = self.vdrain(inc=("s_vec", 1))
                P.op("pe", lambda h: h.matmul(
                    self.pb[6][:, :], self.selab[:, :], self.rcp[0:65, :],
                    start=True, stop=True),
                    waits=[("s_vec", m_rcp)], inc=("s_pe", 1))
                self.rcp_guard = ("s_pe", P.c("s_pe"))
                m_bc = P.c("s_pe")
                bw = [("s_pe", m_bc)]
                g = getattr(self, "bcs_guard", None)
                if g:
                    bw.append(g)
                P.op("act", lambda h: h.copy(self.bcs[:, :], self.pb[6][:, :]),
                     waits=bw, inc=("s_act", 1))
                m_bcs = P.c("s_act")
                self.pb_guard[6] = ("s_act", m_bcs)
                evw = [("s_act", m_bcs), ("s_pe", self.m_qkv_mm_done)]
                P.op("vec", lambda h, p=pair, q=qcs: h.tensor_tensor(
                    self.x0T[0:64, p * 1024 + q * 512: p * 1024 + (q + 1) * 512],
                    self.pb[4][0:64, :], self.bcs[0:64, :], OP.mult),
                    waits=evw)
                P.op("vec", lambda h, p=pair, q=qcs: h.tensor_tensor(
                    self.x0T[64:128, p * 1024 + q * 512: p * 1024 + (q + 1) * 512],
                    self.pb[5][64:128, :], self.bcs[64:128, :], OP.mult),
                    inc=("s_vec", 1))
                m_ev = P.c("s_vec")
                self.bcs_guard = ("s_vec", m_ev)
                self.pb_guard[4] = ("s_vec", m_ev)
                self.pb_guard[5] = ("s_vec", m_ev)
        self.m_heads_evac = P.c("s_vec")

    def proj_epi(self, s, sl):
        P = self.P
        self.x1w_mark = getattr(self, "x1w_mark", [0] * NCH)
        self.x0f_guard = getattr(self, "x0f_guard", {})
        self.x1c_guard = getattr(self, "x1c_guard", {})
        for tc in range(8):
            ch = s * 16 + sl * 8 + tc
            pbase = 0 if tc % 2 == 0 else 2
            fb = tc % 2
            xw = []
            g = self.x0f_guard.get(fb)
            if g:
                xw.append(g)
            m_x0 = self.sydma(self.x0f[:, fb * 1024:(fb + 1) * 1024],
                              self.xr[ch * 128:(ch + 1) * 128, :], waits=xw)
            w = [("s_vec", self.m_heads_evac), self.pb_guard[pbase], self.pb_guard[pbase + 1]]
            if not hasattr(self, "_proj_w_done"):
                w.append(("d_gp", self.m_wproj))
                self._proj_w_done = True
            for nh in range(2):
                bk = self.pb[pbase + nh]
                for dc in range(8):
                    P.op("pe", lambda h, b=bk, t=tc, n=nh, d=dc: h.matmul(
                        b[:, :],
                        self.x0T[:, d * 1024 + t * 128: d * 1024 + (t + 1) * 128],
                        self.wproj[:, d * 1024 + n * 512: d * 1024 + (n + 1) * 512],
                        start=(d == 0), stop=(d == 7)),
                        waits=w if (nh == 0 and dc == 0) else (),
                        inc=("s_pe", 1) if dc == 7 else None)
            m_proj = P.c("s_pe")
            P.op("vec", lambda h, f=fb: h.tensor_tensor(
                self.xb[:, :], self.x0f[:, f * 1024:(f + 1) * 1024], self.bprojb[:, :], OP.add),
                waits=[("d_sy", m_x0), ("s_vec", self.m_bcast_done)])
            for nh in range(2):
                P.op("vec", lambda h, n=nh, b=pbase: h.tensor_tensor(
                    self.ub[:, n * 512:(n + 1) * 512], self.pb[b + n][:, :],
                    self.xb[:, n * 512:(n + 1) * 512], OP.add),
                    waits=[("s_pe", m_proj)] if nh == 0 else (),
                    inc=("s_vec", 1) if nh == 1 else None)
            m_u = P.c("s_vec")
            self.pb_guard[pbase] = ("s_vec", m_u)
            self.pb_guard[pbase + 1] = ("s_vec", m_u)
            m_rstd = self.ln_stats(P, lambda: self.ub[:, :], self.sqb)
            P.op("vec", lambda h: h.tensor_scalar(
                self.nb[:, :], self.ub[:, :], self.smu[:, :], self.srstd[:, :],
                OP.subtract, OP.mult), waits=[("s_act", m_rstd)])
            P.op("vec", lambda h: h.tensor_tensor(self.nb[:, :], self.nb[:, :], self.gamb[:, :], OP.mult))
            P.op("vec", lambda h: h.tensor_tensor(self.nb[:, :], self.nb[:, :], self.betb[:, :], OP.add))
            P.op("vec", lambda h, f=fb: h.tensor_tensor(
                self.ub[:, :], self.nb[:, :], self.x0f[:, f * 1024:(f + 1) * 1024], OP.add),
                inc=("s_vec", 1))
            m_u2 = P.c("s_vec")
            self.x0f_guard[fb] = ("s_vec", m_u2)
            m_rstd2 = self.ln_stats(P, lambda: self.ub[:, :], self.sqb)
            cb = tc % 2
            cw = [("s_act", m_rstd2)]
            g = self.x1c_guard.get(cb)
            if g:
                cw.append(g)
            P.op("vec", lambda h, c=cb: h.tensor_scalar(
                self.x1c[:, c * 1024:(c + 1) * 1024], self.ub[:, :], self.smu[:, :], self.srstd[:, :],
                OP.subtract, OP.mult), waits=cw, inc=("s_vec", 1))
            m_x1 = P.c("s_vec")
            m_w = self.sydma(self.x1d.ap()[ch * 128:(ch + 1) * 128, :],
                             self.x1c[:, cb * 1024:(cb + 1) * 1024],
                             waits=[("s_vec", m_x1)])
            if "x1" in self.dbg:
                m_w = self.sydma(self.t_dbg["x1"].ap()[ch * 128:(ch + 1) * 128, :],
                                 self.x1c[:, cb * 1024:(cb + 1) * 1024])
            self.x1w_mark[ch] = m_w
            self.x1c_guard[cb] = ("d_sy", m_w)
        self.m_proj_done = P.c("s_pe")

    def probs_chunk(self, ch):
        P = self.P
        if True:
            fb = ch % 2
            w = [("d_sy", self.x1w_mark[ch])]
            g = self.x0f_guard.get(fb)
            if g:
                w.append(g)
            m_l = self.sydma(self.x0f[:, fb * 1024:(fb + 1) * 1024],
                             self.x1d.ap()[ch * 128:(ch + 1) * 128, :], waits=w)
            tw = [("d_sy", m_l), self.pb_guard[4], self.pb_guard[5]]
            for dc in range(8):
                bk = self.pb[4 + dc // 4]
                P.op("pe", lambda h, b=bk, d=dc, f=fb: h.transpose(
                    b[:, (d % 4) * 128:((d % 4) + 1) * 128],
                    self.x0f[:, f * 1024 + d * 128: f * 1024 + (d + 1) * 128],
                    self.identf[:, :]),
                    waits=tw if dc == 0 else (), inc=("s_pe", 1) if dc == 7 else None)
            m_tp = P.c("s_pe")
            P.op("act", lambda h: h.copy(self.x1tf[:, 0:512], self.pb[4][:, :]),
                 waits=[("s_pe", m_tp)])
            P.op("act", lambda h: h.copy(self.x1tf[:, 512:1024], self.pb[5][:, :]),
                 inc=("s_act", 1))
            m_ev = P.c("s_act")
            self.pb_guard[4] = ("s_act", m_ev)
            self.pb_guard[5] = ("s_act", m_ev)
            self.x0f_guard[fb] = ("s_act", m_ev)
            rb = 6 if ch % 2 == 0 else 3
            lw = [("s_act", m_ev), self.pb_guard[rb]]
            for dc in range(8):
                P.op("pe", lambda h, d=dc, b=rb: h.matmul(
                    self.pb[b][0:8, 0:128],
                    self.wg[:, d * E:(d + 1) * E],
                    self.x1tf[:, d * 128:(d + 1) * 128],
                    start=(d == 0), stop=(d == 7)),
                    waits=lw if dc == 0 else (), inc=("s_pe", 1) if dc == 7 else None)
            m_log = P.c("s_pe")
            P.op("act", lambda h, b=rb: h.activation(self.expS[:, :], self.pb[b][0:8, 0:128], AF.Exp),
                 waits=[("s_pe", m_log)])
            m_expp = self.adrain(inc=("s_act", 1))
            P.op("pe", lambda h, b=rb: h.matmul(
                self.pb[b][0:1, 128:256], self.ones8col[:, :], self.expS[:, :],
                start=True, stop=True),
                waits=[("s_act", m_expp)], inc=("s_pe", 1))
            m_sum = P.c("s_pe")
            P.op("vec", lambda h, b=rb: h.reciprocal(self.rcpPr[:, :], self.pb[b][0:1, 128:256]),
                 waits=[("s_pe", m_sum)])
            m_rc = self.vdrain(inc=("s_vec", 1))
            P.op("pe", lambda h, b=rb: h.matmul(
                self.pb[b][0:8, 256:384], self.ones1x8[:, :], self.rcpPr[:, :],
                start=True, stop=True),
                waits=[("s_vec", m_rc)], inc=("s_pe", 1))
            m_bc = P.c("s_pe")
            prw = [("s_pe", m_bc)]
            if getattr(self, "m_opdw", 0):
                prw.append(("d_sy", self.m_opdw))
            P.op("vec", lambda h, b=rb: h.tensor_tensor(
                self.prb[:, :], self.expS[:, :], self.pb[b][0:8, 256:384], OP.mult),
                waits=prw)
            m_pr = self.vdrain(inc=("s_vec", 1))
            P.op("pe", lambda h, b=rb: h.matmul(
                self.pb[b][0:64, 384:512], self.ranksel[:, :], self.prb[:, :],
                start=True, stop=True),
                waits=[("s_vec", m_pr)], inc=("s_pe", 1))
            m_pad = P.c("s_pe")
            psw = [("s_pe", m_pad)]
            if getattr(self, "m_padw", 0):
                psw.append(("d_sy", self.m_padw))
            P.op("vec", lambda h, b=rb: h.tensor_copy(self.padstage[:, :], self.pb[b][0:64, 384:512]),
                 waits=psw)
            m_ps = self.vdrain(inc=("s_vec", 1))
            self.pb_guard[rb] = ("s_vec", m_ps)
            wz = [("s_vec", m_ps)]
            if ch == 0:
                wz.append(("d_gp", self.m_cczero))
            self.m_padw = self.sydma(self.cc_in.ap()[:, ch * 128:(ch + 1) * 128],
                                     self.padstage[:, :], waits=wz)
            self.m_opdw = self.sydma(self.opd.ap()[:, ch * 128:(ch + 1) * 128], self.prb[:, :],
                                     waits=[("s_vec", m_pr)])
            if "routing" in self.dbg:
                self.m_opdw = self.sydma(self.t_dbg["probs"].ap()[:, ch * 128:(ch + 1) * 128],
                                         self.prb[:, :])
        self.m_pads_done = P.c("d_sy")

    def routing(self):
        P = self.P
        P.op("gp", lambda h: h.collective_compute(
            "AllReduce", OP.add,
            replica_groups=[list(range(NC_N))],
            ins=[self.cc_in.ap().opt()],
            outs=[self.cc_out.ap().opt()]),
            waits=[("d_sy", self.m_pads_done)], inc=("s_cc", 1))
        m_cc = P.c("s_cc")
        if "bis" in self.dbg:
            self.sydma(self.t_dbg["ccin"].ap()[:, :], self.cc_in.ap()[:, :],
                       waits=[("s_cc", m_cc)])
            self.sydma(self.t_dbg["ccout"].ap()[:, :], self.cc_out.ap()[:, :])
        gath = self.acc[:, 0:4096]
        src = self.cc_out.ap().rearrange("(c e) (hh t) -> (c e hh) t", c=8, e=8, hh=2)
        m_g = self.sydma(gath, src, waits=[("s_cc", m_cc), ("s_pe", self.attn_pe_end)])
        if "bis" in self.dbg:
            m_g = self.sydma(self.t_dbg["gath"].ap()[:, :], gath, waits=[("d_sy", m_g)])
        m_oa = self.sydma(self.acc[0:8, 4096:8192], self.opd.ap()[:, 0:4096],
                          waits=[("d_sy", self.m_pads_done)])
        m_ob = self.sydma(self.acc[32:40, 4096:8192], self.opd.ap()[:, 4096:8192])
        scr = self.fob
        for it in range(BISECT_ITERS):
            w = [("d_sy", m_g), ("s_vec", self.m_prelvec)] if it == 0 else []
            P.op("vec", lambda h: h.tensor_scalar(
                scr[:, 0:2048], self.acc[:, 0:2048], self.midb[:, :], 1.0, OP.is_ge, OP.mult),
                waits=w)
            P.op("vec", lambda h: h.reduce_sum(self.part128[:, :], scr[:, 0:2048], axis=AX.X))
            P.op("vec", lambda h: h.tensor_scalar(
                scr[:, 0:2048], self.acc[:, 2048:4096], self.midb[:, :], 1.0, OP.is_ge, OP.mult))
            P.op("vec", lambda h: h.reduce_sum(self.s1[:, :], scr[:, 0:2048], axis=AX.X))
            self.vdrain()
            P.op("vec", lambda h: h.tensor_tensor(
                self.part128[:, :], self.part128[:, :], self.s1[:, :], OP.add))
            m_part = self.vdrain(inc=("s_vec", 1))
            P.op("pe", lambda h: h.matmul(
                self.pb[6][0:8, 0:1], self.selcnt[:, :], self.part128[:, :],
                start=True, stop=True),
                waits=[("s_vec", m_part)], inc=("s_pe", 1))
            m_cm = P.c("s_pe")
            if it == 0 and "bis" in self.dbg:
                P.op("vec", lambda h: h.tensor_copy(self.dbgc[:, 0:1], self.part128[:, :]))
                P.op("vec", lambda h: h.tensor_copy(self.dbgc[:, 3:4], self.midb[:, :]))
                P.op("vec", lambda h: h.tensor_copy(self.dbgc[0:8, 1:2], self.pb[6][0:8, 0:1]),
                     waits=[("s_pe", m_cm)], inc=("s_vec", 1))
                self.sydma(self.t_dbg["cnt"].ap()[:, :], self.dbgc[:, :],
                           waits=[("s_vec", P.c("s_vec"))])
            P.op("vec", lambda h: h.tensor_scalar(
                self.ge8[:, :], self.pb[6][0:8, 0:1], CAP, None, OP.is_ge),
                waits=[("s_pe", m_cm)])
            P.op("vec", lambda h: h.tensor_tensor(self.tb1[:, :], self.mid8[:, :], self.lo8[:, :], OP.subtract))
            P.op("vec", lambda h: h.tensor_tensor(self.tb2[:, :], self.hi8[:, :], self.mid8[:, :], OP.subtract))
            self.vdrain()
            P.op("vec", lambda h: h.scalar_tensor_tensor(
                self.lo8[:, :], self.tb1[:, :], self.ge8[:, :], self.lo8[:, :], OP.mult, OP.add))
            self.vdrain()
            P.op("vec", lambda h: h.scalar_tensor_tensor(
                self.hi8[:, :], self.tb2[:, :], self.ge8[:, :], self.mid8[:, :], OP.mult, OP.add))
            self.vdrain()
            P.op("vec", lambda h: h.tensor_tensor(self.mid8[:, :], self.lo8[:, :], self.hi8[:, :], OP.add))
            self.vdrain()
            P.op("vec", lambda h: h.tensor_scalar(self.mid8[:, :], self.mid8[:, :], 0.5, None, OP.mult))
            m_mid = self.vdrain(inc=("s_vec", 1))
            P.op("pe", lambda h: h.matmul(
                self.pb[6][0:128, 8:9], self.selbc[:, :], self.mid8[:, :],
                start=True, stop=True),
                waits=[("s_vec", m_mid)], inc=("s_pe", 1))
            m_mb = P.c("s_pe")
            P.op("vec", lambda h: h.tensor_copy(self.midb[:, :], self.pb[6][0:128, 8:9]),
                 waits=[("s_pe", m_mb)])
            self.vdrain(inc=("s_vec", 1))
        P.op("vec", lambda h: h.tensor_scalar(
            self.diag8[:, :], self.eye8[:, :], self.lo8[:, :], None, OP.mult))
        m_diag = self.vdrain(inc=("s_vec", 1))
        P.op("pe", lambda h: h.matmul(
            self.pb[6][0:128, 16:24], self.ones8x128[:, :], self.diag8[:, :],
            start=True, stop=True),
            waits=[("s_vec", m_diag)], inc=("s_pe", 1))
        m_thr = P.c("s_pe")
        P.op("vec", lambda h: h.tensor_copy(self.thrb[:, :], self.pb[6][0:128, 16:24]),
             waits=[("s_pe", m_thr)])
        self.vdrain(inc=("s_vec", 1))
        if "routing" in self.dbg:
            self.sydma(self.t_dbg["thr"].ap()[:, :], self.lo8[:, :],
                       waits=[("s_vec", P.c("s_vec"))])
        for ch in range(NCH):
            if ch < 32:
                srcp = self.acc[0:8, 4096 + ch * 128: 4096 + (ch + 1) * 128]
                idsl = self.identf[0:8, 0:8]
                dw = [("d_sy", m_oa)]
            else:
                srcp = self.acc[32:40, 4096 + (ch - 32) * 128: 4096 + (ch - 31) * 128]
                idsl = self.identf[32:40, 32:40]
                dw = [("d_sy", m_ob)]
            dw.append(("s_vec", P.c("s_vec")))
            P.op("pe", lambda h, sp=srcp, ii=idsl: h.transpose(
                self.pb[6][0:128, 24:32], sp, ii),
                waits=dw, inc=("s_pe", 1))
            m_t = P.c("s_pe")
            P.op("vec", lambda h: h.tensor_copy(self.dtile[:, :], self.pb[6][0:128, 24:32]),
                 waits=[("s_pe", m_t)], inc=("s_vec", 1))
            self.vdrain()
            P.op("vec", lambda h: h.tensor_tensor(self.dtile2[:, :], self.dtile[:, :], self.thrb[:, :], OP.is_ge))
            self.vdrain()
            P.op("vec", lambda h: h.tensor_tensor(self.dtile2[:, :], self.dtile2[:, :], self.dtile[:, :], OP.add))
            self.vdrain()
            P.op("vec", lambda h: h.reduce_max(self.mmax[:, :], self.dtile2[:, :], axis=AX.X))
            self.vdrain()
            P.op("vec", lambda h: h.tensor_scalar(self.dtile2[:, :], self.dtile2[:, :], self.mmax[:, :], None, OP.is_ge))
            self.vdrain()
            P.op("vec", lambda h: h.scalar_tensor_tensor(
                self.dtile3[:, :], self.dtile2[:, :], -1000.0, self.iotap[:, :], OP.mult, OP.add))
            self.vdrain()
            P.op("vec", lambda h: h.tensor_reduce(self.widx[:, :], self.dtile3[:, :], axis=AX.X, op=OP.min))
            self.vdrain()
            P.op("vec", lambda h: h.tensor_scalar(self.dtile2[:, :], self.iota8[:, :], self.widx[:, :], None, OP.is_equal))
            self.vdrain()
            P.op("vec", lambda h: h.tensor_tensor(self.dtile3[:, :], self.dtile[:, :], self.dtile2[:, :], OP.mult))
            self.vdrain()
            P.op("vec", lambda h: h.reduce_sum(self.wval[:, :], self.dtile3[:, :], axis=AX.X))
            self.vdrain()
            P.op("vec", lambda h, c=ch: h.tensor_scalar(
                self.gates[:, c * E:(c + 1) * E], self.dtile2[:, :], self.wval[:, :], None, OP.mult),
                inc=("s_vec", 1))
        self.m_gates = P.c("s_vec")
        if "routing" in self.dbg:
            self.sydma(self.t_dbg["gates"].ap()[:, :], self.gates[:, :],
                       waits=[("s_vec", self.m_gates)])

    def ffn_prep(self, st):
        P = self.P
        self.x1bf_guard = getattr(self, "x1bf_guard", {})
        for chl in range(8):
            ch = st * 8 + chl
            fb = chl % 2
            w = [("d_sy", self.x1w_mark[ch])]
            g = self.x1bf_guard.get(fb)
            if g:
                w.append(g)
            elif chl < 2:
                w.append(("s_pe", self.attn_pe_end))
            m_l = self.gpdma(self.x1bf[:, fb * 1024:(fb + 1) * 1024],
                             self.x1d.ap()[ch * 128:(ch + 1) * 128, :], waits=w)
            tw = [("d_gp", m_l), self.pb_guard[7]]
            for dc in range(8):
                P.op("pe", lambda h, f=fb, d=dc: h.transpose(
                    self.tpb[:, d * 128:(d + 1) * 128],
                    self.x1bf[:, f * 1024 + d * 128: f * 1024 + (d + 1) * 128],
                    self.identb[:, :]),
                    waits=tw if dc == 0 else (), inc=("s_pe", 1) if dc == 7 else None)
            m_tp = P.c("s_pe")
            self.x1bf_guard[fb] = ("s_pe", m_tp)
            P.op("act", lambda h, c=chl: h.copy(
                self.x1T.ap().rearrange("p (d n) -> p d n", d=8)[:, :, c * 128:(c + 1) * 128],
                self.tpb.ap().rearrange("p (d n) -> p d n", d=8)[:, :, :]),
                waits=[("s_pe", m_tp)], inc=("s_act", 1))
            self.pb_guard[7] = ("s_act", P.c("s_act"))
        m_x1T = P.c("s_act")
        if "ffn" in self.dbg and st == 0:
            self.gpdma(self.t_dbg["x1T"].ap()[:, :], self.x1T[:, :],
                       waits=[("s_act", m_x1T)])
        self.m_x1T = m_x1T

    def ffn_body(self, st, mid=None):
        P = self.P
        m_x1T = self.m_x1T
        for e in range(E):
            Xg = getattr(self, "X_guard", None)
            w = [Xg] if Xg else [("s_pe", self.attn_pe_end)]
            mX = 0
            for dc in range(8):
                mX = self.gpdma(self.Xw[:, dc * FF:(dc + 1) * FF],
                                self.w1r[e * D + dc * 128: e * D + (dc + 1) * 128, :],
                                waits=w if dc == 0 else ())
            Yg = getattr(self, "Y_guard", None)
            w = [Yg] if Yg else [("s_pe", self.attn_pe_end)]
            mY = 0
            for fc in range(16):
                mY = self.gpdma(self.Yw[:, fc * D:(fc + 1) * D],
                                self.w2r[e * FF + fc * 128: e * FF + (fc + 1) * 128, :],
                                waits=w if fc == 0 else ())
            h1g = getattr(self, "h1T_guard", None)
            for fc in range(16):
                for th in range(2):
                    bki = (fc * 2 + th) % 2
                    w = [self.pb_guard[bki]]
                    if fc == 0 and th == 0:
                        w += [("d_gp", mX), ("s_act", m_x1T)]
                        if h1g:
                            w.append(h1g)
                    for dc in range(8):
                        P.op("pe", lambda h, b=bki, f=fc, t=th, d=dc: h.matmul(
                            self.pb[b][:, :],
                            self.Xw[:, d * FF + f * 128: d * FF + (f + 1) * 128],
                            self.x1T[:, d * 1024 + t * 512: d * 1024 + (t + 1) * 512],
                            start=(d == 0), stop=(d == 7)),
                            waits=w if dc == 0 else (), inc=("s_pe", 1) if dc == 7 else None)
                        w = w if dc == 0 else w
                    mm = P.c("s_pe")
                    P.op("act", lambda h, b=bki, f=fc, t=th: h.activation(
                        self.h1T[:, f * 1024 + t * 512: f * 1024 + (t + 1) * 512],
                        self.pb[b][:, :], AF.Gelu),
                        waits=[("s_pe", mm)], inc=("s_act", 1))
                    self.pb_guard[bki] = ("s_act", P.c("s_act"))
            m_h1_mm = P.c("s_pe")
            m_h1 = P.c("s_act")
            if "ffn" in self.dbg and st == 0 and e == 0:
                self.gpdma(self.t_dbg["h1T"].ap()[:, :], self.h1T[:, :],
                           waits=[("s_act", m_h1)])
            self.X_guard = ("s_pe", m_h1_mm)
            if e == 0 and mid is not None:
                mid()
            if e == E - 1 and st < 7:
                self.ffn_prep(st + 1)
            for tc in range(8):
                for nh in range(2):
                    bki = 2 + (tc * 2 + nh) % 4
                    w = [self.pb_guard[bki]]
                    if tc == 0 and nh == 0:
                        w += [("d_gp", mY), ("s_act", m_h1)]
                        if e == 0:
                            w.append(("s_vec", self.m_gates))
                    for fc in range(16):
                        P.op("pe", lambda h, b=bki, t=tc, n=nh, f=fc: h.matmul(
                            self.pb[b][:, :],
                            self.h1T[:, f * 1024 + t * 128: f * 1024 + (t + 1) * 128],
                            self.Yw[:, f * 1024 + n * 512: f * 1024 + (n + 1) * 512],
                            start=(f == 0), stop=(f == 15)),
                            waits=w if fc == 0 else (), inc=("s_pe", 1) if fc == 15 else None)
                    mm = P.c("s_pe")
                    gc = (st * 8 + tc) * E + e
                    if e == 0:
                        P.op("vec", lambda h, b=bki, t=tc, n=nh, g2=gc: h.tensor_scalar(
                            self.acc[:, t * 1024 + n * 512: t * 1024 + (n + 1) * 512],
                            self.pb[b][:, :], self.gates[:, g2:g2 + 1], None, OP.mult),
                            waits=[("s_pe", mm)], inc=("s_vec", 1))
                    else:
                        P.op("vec", lambda h, b=bki, t=tc, n=nh, g2=gc: h.scalar_tensor_tensor(
                            self.acc[:, t * 1024 + n * 512: t * 1024 + (n + 1) * 512],
                            self.pb[b][:, :], self.gates[:, g2:g2 + 1],
                            self.acc[:, t * 1024 + n * 512: t * 1024 + (n + 1) * 512],
                            OP.mult, OP.add),
                            waits=[("s_pe", mm)], inc=("s_vec", 1))
                    self.pb_guard[bki] = ("s_vec", P.c("s_vec"))
            m_h2_mm = P.c("s_pe")
            self.h1T_guard = ("s_pe", m_h2_mm)
            self.Y_guard = ("s_pe", m_h2_mm)
        m_acc = P.c("s_vec")
        if "ffn" in self.dbg and st == 0:
            m_dump = self.gpdma(self.t_dbg["accd"].ap()[:, :], self.acc[:, :],
                                waits=[("s_vec", m_acc)])
            self.P.op("vec", lambda h: h.engine_nop(), waits=[("d_gp", m_dump)],
                      inc=("s_vec", 1))
            m_acc = P.c("s_vec")

        self.fx1_guard = getattr(self, "fx1_guard", None)
        self.fob_guard = getattr(self, "fob_guard", {})
        for tc in range(8):
            ch = st * 8 + tc
            w = [("d_sy", self.x1w_mark[ch])]
            if self.fx1_guard:
                w.append(self.fx1_guard)
            m_l = self.sydma(self.fx1f[:, :], self.x1d.ap()[ch * 128:(ch + 1) * 128, :], waits=w)
            P.op("vec", lambda h, t=tc: h.tensor_tensor(
                self.acc[:, t * 1024:(t + 1) * 1024],
                self.acc[:, t * 1024:(t + 1) * 1024], self.fx1f[:, :], OP.add),
                waits=[("d_sy", m_l)], inc=("s_vec", 1))
            m_u = P.c("s_vec")
            self.fx1_guard = ("s_vec", m_u)
            m_rstd = self.ln_stats(P, lambda t=tc: self.acc[:, t * 1024:(t + 1) * 1024], self.fsq)
            ob = tc % 2
            w2 = [("s_act", m_rstd)]
            g = self.fob_guard.get(ob)
            if g:
                w2.append(g)
            P.op("vec", lambda h, t=tc, o=ob: h.tensor_scalar(
                self.fob[:, o * 1024:(o + 1) * 1024],
                self.acc[:, t * 1024:(t + 1) * 1024], self.smu[:, :], self.srstd[:, :],
                OP.subtract, OP.mult), waits=w2, inc=("s_vec", 1))
            m_o = P.c("s_vec")
            m_w = self.sydma(self.yr[ch * 128:(ch + 1) * 128, :],
                             self.fob[:, ob * 1024:(ob + 1) * 1024],
                             waits=[("s_vec", m_o)])
            self.fob_guard[ob] = ("d_sy", m_w)

    def finale(self):
        P = self.P
        P.op("sy", None, waits=[("d_sy", P.c("d_sy")), ("d_gp", P.c("d_gp")),
                                ("d_sc", P.c("d_sc"))])
        P.op("gp", None, waits=[("d_gp", P.c("d_gp"))])

    def build(self):
        self.plan()
        for eng in ENGS:
            for fn, waits, inc in self.P.stream[eng]:
                for s, n in waits:
                    assert n <= self.P.c(s), (eng, s, n, self.P.c(s))
        self.P.emit(self.nc, self.semhandles)
        return self.nc


def _build(dbg=()):
    b = Builder(dbg=dbg)
    nc = b.build()
    return nc, b


def _in_maps(inputs):
    x = np.ascontiguousarray(inputs["L0_attn_target"], dtype=np.float32)
    maps = []
    for c in range(NC_N):
        ranksel = np.zeros((E, 64), dtype=np.float32)
        for e in range(E):
            ranksel[e, c * 8 + e] = 1.0
        maps.append({
            "x": np.ascontiguousarray(x[c * BPC:(c + 1) * BPC]),
            "wqkv": np.ascontiguousarray(inputs["L0_attn_w_qkv"], dtype=np.float32),
            "wproj": np.ascontiguousarray(inputs["L0_attn_w_proj"], dtype=np.float32),
            "bproj": np.ascontiguousarray(inputs["L0_attn_b_proj"], dtype=np.float32).reshape(1, D),
            "gam": np.ascontiguousarray(inputs["L0_attn_ln_gamma"], dtype=np.float32).reshape(1, D),
            "bet": np.ascontiguousarray(inputs["L0_attn_ln_beta"], dtype=np.float32).reshape(1, D),
            "wg": np.ascontiguousarray(inputs["L0_ffn_Wg"], dtype=np.float32),
            "w1": np.ascontiguousarray(inputs["L0_ffn_W1"], dtype=np.float32),
            "w2": np.ascontiguousarray(inputs["L0_ffn_W2"], dtype=np.float32),
            "ranksel": ranksel,
        })
    return maps


def _run(inputs, dbg=()):
    nc, b = _build(dbg=dbg)
    maps = _in_maps(inputs)
    res = run_bass_kernel_spmd(nc, maps, core_ids=list(range(NC_N)))
    return res


def kernel(**inputs):
    res = _run(inputs)
    out = np.concatenate([np.asarray(res.results[c]["y"]) for c in range(NC_N)], axis=0)
    return out.astype(np.float32)
